# version 16
# baseline (speedup 1.0000x reference)
import numpy as np
import ml_dtypes
from contextlib import ExitStack

import concourse.bass as bass
import concourse.mybir as mybir
from concourse.bass_utils import run_bass_kernel_spmd

F32 = mybir.dt.float32
FP8 = mybir.dt.float8e4
BF16 = mybir.dt.bfloat16
I32 = mybir.dt.int32
AF = mybir.ActivationFunctionType
ALU = mybir.AluOpType
AX = mybir.AxisListType

PI = float(np.pi)
LN_EPS = 1e-5
COS_EPS = 1e-6

NCORES = 8
B = 16
T = 4096
D = 1024
KD = D // 128
NH = 16
DH = 64
DM = 4096
B_PER_CORE = B // NCORES
NTOK = B_PER_CORE * T
CH = 256
TILES = CH // 128
NCHUNK = NTOK // CH
CHB = T // CH
MG = 128
NMG = DM // MG

ENG_SEM = {"tensor": "pe", "vector": "dve", "scalar": "act"}


class Buf:
    __slots__ = ("name", "ready", "readers", "war")

    def __init__(self, name, war=True):
        self.name = name
        self.ready = None
        self.readers = []
        self.war = war


class Prog:
    ENGS = ("sync", "tensor", "vector", "scalar", "gpsimd")

    def __init__(self):
        self.items = {e: [] for e in self.ENGS}
        self.cnt = {}
        self.marks = {}

    def _wait(self, eng, cond):
        s, v = cond
        if self.marks.get((eng, s), 0) >= v:
            return
        self.marks[(eng, s)] = v
        self.items[eng].append(("w", s, v))

    def _ready_conds(self, b):
        if b.ready is None:
            return ()
        if isinstance(b.ready, list):
            return b.ready
        return (b.ready,)

    def _prewait(self, eng, reads, writes):
        for b in reads:
            for c in self._ready_conds(b):
                self._wait(eng, c)
        for b in writes:
            for c in self._ready_conds(b):
                self._wait(eng, c)
            if b.war:
                for c in b.readers:
                    self._wait(eng, c)

    def _post(self, cond, reads, writes):
        for b in writes:
            b.ready = cond
            b.readers = []
        for b in reads:
            if b.war:
                b.readers.append(cond)

    def op(self, eng, fn, reads=(), writes=(), sem=None, k=1):
        if sem is None:
            sem = ENG_SEM[eng]
        self._prewait(eng, reads, writes)
        self.cnt[sem] = self.cnt.get(sem, 0) + k
        cond = (sem, self.cnt[sem])
        self.items[eng].append(("o", fn, sem, k))
        self._post(cond, reads, writes)
        return cond

    def group(self, eng, fns, reads=(), writes=(), sem=None):
        if sem is None:
            sem = ENG_SEM[eng]
        self._prewait(eng, reads, writes)
        for fn in fns[:-1]:
            self.items[eng].append(("o", fn, None, 0))
        self.cnt[sem] = self.cnt.get(sem, 0) + 1
        cond = (sem, self.cnt[sem])
        self.items[eng].append(("o", fns[-1], sem, 1))
        self._post(cond, reads, writes)
        return cond


def _prep_inputs(inputs):
    f32 = np.float32
    W_Q = np.asarray(inputs["L0_attn_W_Q"], f32)
    W_K = np.asarray(inputs["L0_attn_W_K"], f32)
    W_V = np.asarray(inputs["L0_attn_W_V"], f32)
    W_O = np.asarray(inputs["L0_attn_W_O"], f32)
    wqkv = np.ascontiguousarray(np.concatenate(
        [W_V.transpose(1, 2, 0).reshape(D, NH * DH),
         W_Q[:, :, 0].T, W_K[:, :, 0].T], axis=1))
    wot = np.ascontiguousarray(W_O.reshape(D, DH * NH).T)
    w1 = np.ascontiguousarray(np.asarray(inputs["L0_attn_W1"], f32))
    w2 = np.ascontiguousarray(np.asarray(inputs["L0_attn_W2"], f32))
    ln1w_c = np.ascontiguousarray(np.asarray(inputs["L0_attn_ln1_w"], f32).reshape(KD, 128).T)
    ln1b_c = np.ascontiguousarray(np.asarray(inputs["L0_attn_ln1_b"], f32).reshape(KD, 128).T)
    ln2w_c = np.ascontiguousarray(np.asarray(inputs["L0_attn_ln2_w"], f32).reshape(KD, 128).T)
    ln2b_c = np.ascontiguousarray(np.asarray(inputs["L0_attn_ln2_b"], f32).reshape(KD, 128).T)
    b1c = np.ascontiguousarray(np.asarray(inputs["L0_attn_b1"], f32).reshape(DM // 128, 128).T)
    b2row = np.ascontiguousarray(np.asarray(inputs["L0_attn_b2"], f32).reshape(1, D))
    fbias = np.ascontiguousarray(
        np.broadcast_to(np.asarray(inputs["L0_ffn_bias"], f32), (128, D)))
    fscale = np.ascontiguousarray(
        np.broadcast_to(np.asarray(inputs["L0_ffn_weight_scale"], f32).reshape(1, 1), (128, 1)))
    packed = np.ascontiguousarray(np.asarray(inputs["L0_ffn_packed_weight"], np.int32).reshape(256, 1024))
    ident = np.eye(128, dtype=f32)
    sel = np.ascontiguousarray(np.tile(np.eye(NH, dtype=f32), (1, 128 // NH)))
    common = dict(wqkv=wqkv, wot=wot, w1=w1, w2=w2, ln1w_c=ln1w_c, ln1b_c=ln1b_c,
                  ln2w_c=ln2w_c, ln2b_c=ln2b_c, b1c=b1c, b2row=b2row, fbias=fbias,
                  fscale=fscale, packed=packed, ident=ident, sel=sel)
    x_full = np.asarray(inputs["L0_attn_x"], f32)
    in_maps = []
    for j in range(NCORES):
        m = dict(common)
        m["x"] = np.ascontiguousarray(
            x_full[j * B_PER_CORE:(j + 1) * B_PER_CORE].reshape(NTOK, D))
        in_maps.append(m)
    return in_maps


def build_program():
    nc = bass.Bass("TRN2", target_bir_lowering=False)
    P = Prog()

    dr = {}
    for name, shape, dt in [
        ("x", [NTOK, D], F32), ("wqkv", [D, 1056], F32), ("wot", [D, D], F32),
        ("w1", [D, DM], F32), ("w2", [DM, D], F32),
        ("ln1w_c", [128, KD], F32), ("ln1b_c", [128, KD], F32),
        ("ln2w_c", [128, KD], F32), ("ln2b_c", [128, KD], F32),
        ("b1c", [128, DM // 128], F32), ("b2row", [1, D], F32),
        ("fbias", [128, D], F32), ("fscale", [128, 1], F32),
        ("packed", [256, 1024], I32), ("ident", [128, 128], F32),
        ("sel", [NH, 128], F32),
    ]:
        dr[name] = nc.dram_tensor(name, shape, dt, kind="ExternalInput").ap()
    out_d = nc.dram_tensor("out", [NTOK, D], F32, kind="ExternalOutput").ap()
    scr = {}
    for name, shape in [("w1s", [D, 2 * DM]), ("w2s", [DM, 2 * D])]:
        scr[name] = nc.dram_tensor(name, shape, BF16, kind="Internal").ap()

    stack = ExitStack()

    def sb(name, shape, dt=F32):
        return stack.enter_context(nc.sbuf_tensor(name, shape, dt))

    with stack:
        ident = sb("identS", [128, 128])
        sel = sb("selS", [NH, 128])
        wqkvh = sb("wqkvhS", [128, KD * 1056], BF16)
        wqkvl = sb("wqkvlS", [128, KD * 1056], BF16)
        woth = sb("wothS", [128, KD * D], BF16)
        wotl = sb("wotlS", [128, KD * D], BF16)
        ln1w = sb("ln1wS", [128, KD]); ln1b = sb("ln1bS", [128, KD])
        ln2w = sb("ln2wS", [128, KD]); ln2b = sb("ln2bS", [128, KD])
        b1c = sb("b1cS", [128, DM // 128])
        b2row = sb("b2rowS", [1, D])
        fbias = sb("fbiasS", [128, D])
        fscale = sb("fscaleS", [128, 1])
        onesrow = sb("onesrowS", [1, 128])
        ternT = sb("ternTS", [128, KD * D], FP8)
        p2h = sb("p2hS", [128, 1024], I32)
        onesN = sb("onesNS", [128, CH])
        epsc = sb("epscS", [128, 1])
        eps2c = sb("eps2cS", [128, 1])
        halfpi = sb("halfpiS", [128, 1])
        xb = sb("xbS", [128, TILES * D])
        w1gc = [sb(f"w1gcS{i}", [128, KD * 2 * MG], BF16) for i in range(2)]
        w2gc = [sb(f"w2gcS{i}", [128, 2 * D], BF16) for i in range(4)]
        ab = sb("abS", [128, TILES * D])
        x1b = sb("x1bS", [128, TILES * D])
        y1T2 = sb("y1T2S", [128, KD * 2 * CH], BF16)
        y2T2 = sb("y2T2S", [128, KD * 2 * CH], BF16)
        tscr = sb("tscrS", [128, 512])
        ytmp = sb("ytmpS", [128, D])
        tmpA = sb("tmpAS", [128, D])
        hgh = [sb(f"hghS{i}", [128, CH], BF16) for i in range(3)]
        hgl = [sb(f"hglS{i}", [128, CH], BF16) for i in range(3)]
        hgf = sb("hgfS", [128, CH])
        hh2 = sb("hh2S", [128, KD * CH], BF16)
        hl2 = sb("hl2S", [128, KD * CH], BF16)
        hdf = sb("hdfS", [128, CH])
        kvc = sb("kvcS", [128, CH]); kvs = sb("kvsS", [128, CH])
        mcos = sb("mcosS", [128, CH]); msin = sb("msinS", [128, CH])
        t1b = sb("t1bS", [128, CH]); t2b = sb("t2bS", [128, CH])
        coskb = sb("coskbS", [128, CH]); sinkb = sb("sinkbS", [128, CH])
        cosqb = sb("cosqbS", [128, CH]); sinqb = sb("sinqbS", [128, CH])
        carc = sb("carcS", [128, KD]); cars = sb("carsS", [128, KD])
        xqb = sb("xqbS", [128, TILES * D])
        xqT = sb("xqTS", [128, KD * CH], BF16)
        i32s = sb("i32sS", [128, 512], I32)
        coskr = sb("coskrS", [NH, CH]); sinkr = sb("sinkrS", [NH, CH])
        cq = sb("cqS", [NH, CH]); sq = sb("sqS", [NH, CH])
        ckb = sb("ckbS", [NH, 2 * CH])
        carry2 = sb("carry2S", [NH, 2])
        bnb = sb("bnbS", [128, 12])
        mv = sb("mvS", [128, 2])
        fbnb = sb("fbnbS", [128, 12])
        fmv = sb("fmvS", [128, 2])
        fs1c = sb("fs1cS", [128, 1]); fsdc = sb("fsdcS", [128, 1])
        fr1t = sb("fr1tS", [128, TILES]); fv1t = sb("fv1tS", [128, TILES])
        rct = sb("rctS", [128, TILES]); nmt = sb("nmtS", [128, TILES])
        sdc = sb("sdcS", [128, 1]); rc = sb("rcS", [128, 1])
        nmc = sb("nmcS", [128, 1]); v1c = sb("v1cS", [128, 1])
        s1c = sb("s1cS", [128, 1]); r1c = sb("r1cS", [128, 1])
        amaxc = sb("amaxcS", [128, 1]); mxc = sb("mxcS", [128, 1])
        rcpmc = sb("rcpmcS", [128, 1]); ascc = sb("asccS", [128, 1])
        rsc = [sb(f"rscS{i}", [128, 1]) for i in range(TILES)]
        qkt = x1b[0:NH, 0:2 * CH]
        scnt = x1b[0:NH, 2 * CH:4 * CH]

        pb = [stack.enter_context(nc.psum_tensor(f"pb{i}", [128, 512], F32))
              for i in range(8)]

        sems = {n: stack.enter_context(nc.semaphore(n))
                for n in ("pe", "dve", "act", "gp", "qc", "qp", "qx", "qx2",
                          "qsp", "qsl", "qsp2", "qsl2",
                          "w1ah", "w1al", "w1bh", "w1bl",
                          "w2ah", "w2al", "w2bh", "w2bl", "w2ch", "w2cl",
                          "w2dh", "w2dl", "qo", "qo2")}

        bufs = {}

        def BF(name, war=True):
            bufs[name] = Buf(name, war)
            return bufs[name]

        const_names = ["ident", "sel", "ln1w", "ln1b",
                       "ln2w", "ln2b", "b1c", "b2row", "fbias", "fscale",
                       "onesrow", "onesN", "epsc", "eps2c", "halfpi",
                       "w1hD", "w1lD", "w2hD", "w2lD"]
        for n in const_names:
            BF(n, war=False)
        for n in ["wqkvh", "wqkvl", "woth", "wotl", "tscr", "hgf", "hdf",
                  "ternT", "p2h", "xb", "ab", "x1b", "y1T2", "y2T2", "ytmp",
                  "fbnb", "fmv", "fs1c", "fsdc", "fr1t", "fv1t", "rct", "nmt",
                  "tmpA", "kvc", "kvs", "mcos", "msin", "t1b", "t2b",
                  "coskb", "sinkb", "cosqb", "sinqb", "carc", "cars",
                  "xqb", "xqT", "i32s", "coskr", "sinkr", "cq", "sq",
                  "ckb", "carry2",
                  "bnb", "mv", "sdc", "rc", "nmc", "v1c", "s1c", "r1c",
                  "amaxc", "mxc", "rcpmc", "ascc"]:
            BF(n)
        BF("hh2"); BF("hl2")
        for i in range(2):
            BF(f"w1gc{i}")
        for i in range(3):
            BF(f"hgh{i}"); BF(f"hgl{i}")
        for i in range(4):
            BF(f"w2gc{i}")
        for i in range(TILES):
            BF(f"rsc{i}")
        for i in range(8):
            BF(f"pb{i}")

        def b(n):
            return bufs[n]

        def dma(eng, dst_ap, src_ap, reads=(), writes=(), sem=None):
            P.op(eng, lambda e, d=dst_ap, s=src_ap: e.dma_start(out=d, in_=s),
                 reads=reads, writes=writes, sem=sem, k=16)

        qg = dict(sem="qc")
        const_bufs = []
        dma("gpsimd", ident[:], dr["ident"], writes=[b("ident")], **qg)
        dma("gpsimd", sel[:], dr["sel"], writes=[b("sel")], **qg)
        for n in ["ln1w_c", "ln1b_c", "ln2w_c", "ln2b_c"]:
            tgt = {"ln1w_c": ln1w, "ln1b_c": ln1b, "ln2w_c": ln2w, "ln2b_c": ln2b}[n]
            dma("gpsimd", tgt[:], dr[n], writes=[b(n[:4])], **qg)
        dma("gpsimd", b1c[:], dr["b1c"], writes=[b("b1c")], **qg)
        dma("gpsimd", b2row[:], dr["b2row"], writes=[b("b2row")], **qg)
        dma("gpsimd", fbias[:], dr["fbias"], writes=[b("fbias")], **qg)
        dma("gpsimd", fscale[:], dr["fscale"], writes=[b("fscale")], **qg)
        for n in ["ident", "sel", "ln1w", "ln1b",
                  "ln2w", "ln2b", "b1c", "b2row", "fbias", "fscale"]:
            bufs[n].ready = ("qc", P.cnt["qc"])

        def split_block(src_dram_ap, nfree, hi_ap, lo_ap, hi_buf, lo_buf,
                        out_h=None, out_l=None):
            dma("sync", xb[:, 0:nfree], src_dram_ap, writes=[b("xb")], sem="qx")
            P.op("scalar", lambda e, hi_ap=hi_ap, nfree=nfree: e.copy(
                out=hi_ap, in_=xb[:, 0:nfree]),
                 reads=[b("xb")], writes=[hi_buf])
            P.op("vector", lambda e, hi_ap=hi_ap, lo_ap=lo_ap, nfree=nfree:
                 e.tensor_tensor(out=lo_ap, in0=xb[:, 0:nfree], in1=hi_ap,
                                 op=ALU.subtract),
                 reads=[b("xb"), hi_buf], writes=[lo_buf])
            if out_h is not None:
                dma("gpsimd", out_h, hi_ap, reads=[hi_buf], sem="qsp")
                dma("gpsimd", out_l, lo_ap, reads=[lo_buf], sem="qsl")

        for k in range(KD):
            split_block(dr["wqkv"][k * 128:(k + 1) * 128, :], 1056,
                        wqkvh[:, k * 1056:(k + 1) * 1056],
                        wqkvl[:, k * 1056:(k + 1) * 1056],
                        b("wqkvh"), b("wqkvl"))
        spx = [Buf("spx0"), Buf("spx1")]
        sph = [Buf("sph0"), Buf("sph1")]
        spl = [Buf("spl0"), Buf("spl1")]
        for bb_ in spx:
            bb_.ready = bufs["xb"].ready
            bb_.readers = list(bufs["xb"].readers)
        for bb_ in sph:
            bb_.ready = bufs["xqT"].ready
        for bb_ in spl:
            bb_.ready = bufs["y1T2"].ready
        blocks = []
        for k in range(KD):
            blocks.append(("wot", dr["wot"][k * 128:(k + 1) * 128, :],
                           woth[:, k * D:(k + 1) * D], wotl[:, k * D:(k + 1) * D],
                           None, None))
        for k in range(KD):
            for hf in range(4):
                base = scr["w1s"][k * 128:(k + 1) * 128, hf * 2 * D:(hf + 1) * 2 * D]
                hi_dst = base.rearrange("p (g m) -> p g m", g=8)[:, :, 0:128]
                lo_dst = base.rearrange("p (g m) -> p g m", g=8)[:, :, 128:256]
                blocks.append(("w1",
                               dr["w1"][k * 128:(k + 1) * 128, hf * D:(hf + 1) * D],
                               None, None, hi_dst, lo_dst))
        for mk in range(DM // 128):
            blocks.append(("w2", dr["w2"][mk * 128:(mk + 1) * 128, :],
                           None, None,
                           scr["w2s"][mk * 128:(mk + 1) * 128, 0:D],
                           scr["w2s"][mk * 128:(mk + 1) * 128, D:2 * D]))
        for i, (kind, src, hdst, ldst, oh, ol) in enumerate(blocks):
            sl = i % 2
            xh_ap = xb[:, sl * D:(sl + 1) * D]
            hi_ap = hdst if hdst is not None else xqT[:, sl * D:(sl + 1) * D]
            lo_ap = ldst if ldst is not None else y1T2[:, sl * D:(sl + 1) * D]
            hi_buf = b("woth") if hdst is not None else sph[sl]
            lo_buf = b("wotl") if ldst is not None else spl[sl]
            dma("sync", xh_ap, src, writes=[spx[sl]], sem="qx" if sl == 0 else "qx2")
            P.op("scalar", lambda e, hi_ap=hi_ap, xh_ap=xh_ap: e.copy(
                out=hi_ap, in_=xh_ap),
                 reads=[spx[sl]], writes=[hi_buf])
            P.op("vector", lambda e, hi_ap=hi_ap, lo_ap=lo_ap, xh_ap=xh_ap:
                 e.tensor_tensor(out=lo_ap, in0=xh_ap, in1=hi_ap, op=ALU.subtract),
                 reads=[spx[sl], hi_buf], writes=[lo_buf])
            if oh is not None:
                hi_src = hi_ap.rearrange("p (g m) -> p g m", g=8) \
                    if kind == "w1" else hi_ap
                lo_src = lo_ap.rearrange("p (g m) -> p g m", g=8) \
                    if kind == "w1" else lo_ap
                dma("gpsimd", oh, hi_src, reads=[hi_buf],
                    sem="qsp" if sl == 0 else "qsp2")
                dma("gpsimd", ol, lo_src, reads=[lo_buf],
                    sem="qsl" if sl == 0 else "qsl2")
        for tgt, parts in (("xb", spx), ("xqT", sph), ("y1T2", spl)):
            rd = []
            for bb_ in parts:
                if bb_.ready is not None:
                    rd.append(bb_.ready)
                rd += bb_.readers
            bufs[tgt].readers += rd
        for n in ["w1hD", "w2hD"]:
            bufs[n].ready = [("qsp", P.cnt.get("qsp", 0)),
                             ("qsp2", P.cnt.get("qsp2", 0)),
                             ("qsl", P.cnt.get("qsl", 0)),
                             ("qsl2", P.cnt.get("qsl2", 0))]

        P.op("vector", lambda e: e.memset(onesN[:], 1.0), writes=[b("onesN")])
        P.op("vector", lambda e: e.memset(epsc[:], LN_EPS), writes=[b("epsc")])
        P.op("vector", lambda e: e.memset(eps2c[:], LN_EPS * LN_EPS), writes=[b("eps2c")])
        P.op("vector", lambda e: e.memset(halfpi[:], PI / 2), writes=[b("halfpi")])
        P.op("vector", lambda e: e.memset(onesrow[:], 1.0), writes=[b("onesrow")])

        for half in range(2):
            dma("gpsimd", p2h[:], dr["packed"][half * 128:(half + 1) * 128, :],
                writes=[b("p2h")], sem="qp")
            for oc2 in range(4):
                oc = oc2 * 2 + half
                s = 2 * (oc // 2)
                toc = xqb[:, 0:D]
                for hv in range(2):
                    P.op("vector",
                         lambda e, s=s, hv=hv: e.tensor_scalar(
                             out=i32s[:], in0=p2h[:, hv * 512:(hv + 1) * 512],
                             scalar1=s, scalar2=3,
                             op0=ALU.logical_shift_right, op1=ALU.bitwise_and),
                         reads=[b("p2h")], writes=[b("i32s")])
                    P.op("vector",
                         lambda e, toc=toc, hv=hv: e.tensor_scalar(
                             out=toc[:, hv * 512:(hv + 1) * 512], in0=i32s[:],
                             scalar1=-1.0, scalar2=None, op0=ALU.add),
                         reads=[b("i32s")], writes=[b("xqb")])
                for half4 in range(2):
                    fns = [
                        (lambda e, k=k, h4=half4, toc=toc: e.transpose(
                            pb[h4][:, (k % 4) * 128:(k % 4 + 1) * 128],
                            toc[:, k * 128:(k + 1) * 128], ident[:]))
                        for k in range(half4 * 4, half4 * 4 + 4)]
                    P.group("tensor", fns, reads=[b("xqb"), b("ident")],
                            writes=[b(f"pb{half4}")])
                    P.op("scalar",
                         lambda e, h4=half4, oc=oc: e.copy(
                             out=ternT[:].rearrange("p (k m) -> p k m", k=KD)
                             [:, h4 * 4:(h4 + 1) * 4, oc * 128:(oc + 1) * 128],
                             in_=pb[h4][:].rearrange("p (k m) -> p k m", k=4)),
                         reads=[b(f"pb{half4}")], writes=[b("ternT")])

        def x_load(c):
            dst = xb[:].rearrange("p (t d) -> p t d", t=TILES)
            src = dr["x"].rearrange("(c t p) d -> c p t d", c=NCHUNK, t=TILES)[c]
            dma("sync", dst, src, writes=[b("xb")], sem="qx")

        def w1_load(g, slot):
            dst = w1gc[slot][:].rearrange("p (k m) -> p k m", k=KD)
            src = scr["w1s"].rearrange("(k p) m -> p k m", k=KD)[
                :, :, g * 2 * MG:(g + 1) * 2 * MG]
            sm = "w1a" if slot == 0 else "w1b"
            dma("sync", dst, src, reads=[b("w1hD")],
                writes=[b(f"w1gc{slot}")], sem=sm + "h")

        def w2_load(g, slot):
            base = {0: "w2a", 1: "w2b", 2: "w2c", 3: "w2d"}[slot]
            dma("gpsimd", w2gc[slot][:], scr["w2s"][g * 128:(g + 1) * 128, :],
                reads=[b("w2hD")], writes=[b(f"w2gc{slot}")], sem=base + "h")

        x_load(0)

        def yh2(t, k):
            return t[:, k * 2 * CH:k * 2 * CH + CH]

        def yl2(t, k):
            return t[:, k * 2 * CH + CH:(k + 1) * 2 * CH]

        def yhl2(t, k):
            return t[:, k * 2 * CH:(k + 1) * 2 * CH]

        def front_part1(c):
            for tt in range(TILES):
                xt = xb[:, tt * D:(tt + 1) * D]
                P.op("vector", lambda e, xt=xt: e.bn_stats(fbnb[:, 0:6], xt[:, 0:512]),
                     reads=[b("xb")], writes=[b("fbnb")])
                P.op("vector", lambda e, xt=xt: e.bn_stats(fbnb[:, 6:12], xt[:, 512:1024]),
                     reads=[b("xb")], writes=[b("fbnb")])
                P.op("vector", lambda e: e.bn_aggr(fmv[:], fbnb[:]),
                     reads=[b("fbnb")], writes=[b("fmv")])
                P.op("scalar", lambda e: e.activation(out=fs1c[:], in_=fmv[:, 1:2],
                                                      func=AF.Sqrt, scale=1.0 + LN_EPS,
                                                      bias=eps2c[:]),
                     reads=[b("fmv"), b("eps2c")], writes=[b("fs1c")])
                P.op("vector", lambda e, tt=tt: e.reciprocal(
                    out=fr1t[:, tt:tt + 1], in_=fs1c[:]),
                     reads=[b("fs1c")], writes=[b("fr1t")])
                P.op("vector", lambda e, tt=tt: e.tensor_scalar(
                    out=fv1t[:, tt:tt + 1], in0=fmv[:, 0:1],
                    scalar1=fr1t[:, tt:tt + 1], scalar2=-1.0,
                    op0=ALU.mult, op1=ALU.mult),
                     reads=[b("fmv"), b("fr1t")], writes=[b("fv1t")])
                P.op("scalar", lambda e: e.activation(out=fsdc[:], in_=fmv[:, 1:2],
                                                      func=AF.Sqrt, bias=epsc[:]),
                     reads=[b("fmv"), b("epsc")], writes=[b("fsdc")])
                P.op("vector", lambda e, tt=tt: e.reciprocal(
                    out=rct[:, tt:tt + 1], in_=fsdc[:]),
                     reads=[b("fsdc")], writes=[b("rct")])
                P.op("vector", lambda e, tt=tt: e.tensor_scalar(
                    out=nmt[:, tt:tt + 1], in0=fmv[:, 0:1],
                    scalar1=rct[:, tt:tt + 1], scalar2=-1.0,
                    op0=ALU.mult, op1=ALU.mult),
                     reads=[b("fmv"), b("rct")], writes=[b("nmt")])

        def front_part2(c):
            lc = c % CHB
            for tt in range(TILES):
                P.op("scalar", lambda e, tt=tt: e.activation(
                    out=ytmp[:], in_=xb[:, tt * D:(tt + 1) * D], func=AF.Identity,
                    scale=fr1t[:, tt:tt + 1], bias=fv1t[:, tt:tt + 1]),
                     reads=[b("xb"), b("fr1t"), b("fv1t")], writes=[b("ytmp")])
                for h4 in range(2):
                    fns = [(lambda e, k=k, h4=h4: e.transpose(
                        pb[h4][:, (k % 4) * 128:(k % 4 + 1) * 128],
                        ytmp[:, k * 128:(k + 1) * 128], ident[:]))
                        for k in range(h4 * 4, h4 * 4 + 4)]
                    P.group("tensor", fns, reads=[b("ytmp"), b("ident")],
                            writes=[b(f"pb{h4}")])
                    for k in range(h4 * 4, h4 * 4 + 4):
                        P.op("scalar", lambda e, k=k, h4=h4: e.activation(
                            out=tscr[:, (k % 4) * 128:(k % 4 + 1) * 128],
                            in_=pb[h4][:, (k % 4) * 128:(k % 4 + 1) * 128],
                            func=AF.Identity, scale=ln1w[:, k:k + 1],
                            bias=ln1b[:, k:k + 1]),
                             reads=[b(f"pb{h4}"), b("ln1w"), b("ln1b")],
                             writes=[b("tscr")])
                    hdst = y1T2[:].rearrange("p (k t) -> p k t", k=KD)[
                        :, h4 * 4:(h4 + 1) * 4, tt * 128:(tt + 1) * 128]
                    ldst = y1T2[:].rearrange("p (k t) -> p k t", k=KD)[
                        :, h4 * 4:(h4 + 1) * 4, CH + tt * 128:CH + (tt + 1) * 128]
                    tsrc = tscr[:].rearrange("p (k t) -> p k t", k=4)
                    P.op("scalar", lambda e, hdst=hdst, tsrc=tsrc: e.copy(
                        out=hdst, in_=tsrc),
                         reads=[b("tscr")], writes=[b("y1T2")])
                    P.op("vector", lambda e, hdst=hdst, ldst=ldst, tsrc=tsrc:
                         e.tensor_tensor(out=ldst, in0=tsrc, in1=hdst, op=ALU.subtract),
                         reads=[b("tscr"), b("y1T2")], writes=[b("y1T2")])
            for qoff, outsl in ((1024, pb[0][0:NH, 0:CH]), (1040, pb[0][0:NH, CH:2 * CH])):
                fns = []
                for k in range(KD):
                    wh_ap = wqkvh[:, k * 1056 + qoff:k * 1056 + qoff + 16]
                    wl_ap = wqkvl[:, k * 1056 + qoff:k * 1056 + qoff + 16]
                    fns.append(lambda e, o=outsl, w=wh_ap, r=yh2(y1T2, k),
                               st=(k == 0): e.matmul(o, w, r, start=st, stop=False))
                    fns.append(lambda e, o=outsl, w=wh_ap, r=yl2(y1T2, k):
                               e.matmul(o, w, r, start=False, stop=False))
                    fns.append(lambda e, o=outsl, w=wl_ap, r=yh2(y1T2, k),
                               sp=(k == KD - 1): e.matmul(o, w, r, start=False, stop=sp))
                P.group("tensor", fns,
                        reads=[b("wqkvh"), b("wqkvl"), b("y1T2")], writes=[b("pb0")])
            P.op("scalar", lambda e: e.activation(out=qkt[:], in_=pb[0][0:NH, 0:2 * CH],
                                                  func=AF.Tanh),
                 reads=[b("pb0")], writes=[b("x1b")])
            P.op("scalar", lambda e: e.activation(out=cq[:], in_=qkt[:, 0:CH],
                                                  func=AF.Sin, scale=PI / 4,
                                                  bias=halfpi[0:NH, :]),
                 reads=[b("x1b"), b("halfpi")], writes=[b("cq")])
            P.op("scalar", lambda e: e.activation(out=sq[:], in_=qkt[:, 0:CH],
                                                  func=AF.Sin, scale=PI / 4),
                 reads=[b("x1b")], writes=[b("sq")])
            P.op("scalar", lambda e: e.activation(out=coskr[:], in_=qkt[:, CH:2 * CH],
                                                  func=AF.Sin, scale=PI / 4,
                                                  bias=halfpi[0:NH, :]),
                 reads=[b("x1b"), b("halfpi")], writes=[b("coskr")])
            P.op("scalar", lambda e: e.activation(out=sinkr[:], in_=qkt[:, CH:2 * CH],
                                                  func=AF.Sin, scale=PI / 4),
                 reads=[b("x1b")], writes=[b("sinkr")])
            if lc != 0:
                P.op("vector", lambda e: e.tensor_copy(
                    out=carry2[:, 0:1], in_=ckb[:, CH - 1:CH]),
                     reads=[b("ckb")], writes=[b("carry2")])
                P.op("vector", lambda e: e.tensor_copy(
                    out=carry2[:, 1:2], in_=ckb[:, 2 * CH - 1:2 * CH]),
                     reads=[b("ckb")], writes=[b("carry2")])
            for half, src in ((0, "coskr"), (1, "sinkr")):
                srct = {"coskr": coskr, "sinkr": sinkr}[src]
                if lc == 0:
                    P.op("vector", lambda e, h=half, st=srct: e.tensor_tensor_scan(
                        out=ckb[:, h * CH:(h + 1) * CH], data0=onesN[0:NH, :],
                        data1=st[:], initial=0.0, op0=ALU.mult, op1=ALU.add),
                         reads=[b("onesN"), b(src)], writes=[b("ckb")])
                else:
                    P.op("vector", lambda e, h=half, st=srct: e.tensor_tensor_scan(
                        out=scnt[:, h * CH:(h + 1) * CH], data0=onesN[0:NH, :],
                        data1=st[:], initial=0.0, op0=ALU.mult, op1=ALU.add),
                         reads=[b("onesN"), b(src)], writes=[b("x1b")])
                    P.op("vector", lambda e, h=half: e.tensor_scalar(
                        out=ckb[:, h * CH:(h + 1) * CH], in0=scnt[:, h * CH:(h + 1) * CH],
                        scalar1=carry2[:, h:h + 1], scalar2=None, op0=ALU.add),
                         reads=[b("x1b"), b("carry2")], writes=[b("ckb")])
            P.op("vector", lambda e: e.tensor_tensor(out=scnt[:, 0:CH], in0=cq[:],
                                                     in1=ckb[:, 0:CH], op=ALU.mult),
                 reads=[b("cq"), b("ckb")], writes=[b("x1b")])
            P.op("vector", lambda e: e.tensor_tensor(out=scnt[:, CH:2 * CH], in0=sq[:],
                                                     in1=ckb[:, CH:2 * CH], op=ALU.mult),
                 reads=[b("sq"), b("ckb")], writes=[b("x1b")])
            P.op("vector", lambda e: e.scalar_tensor_tensor(
                out=qkt[:, 0:CH], in0=scnt[:, 0:CH], scalar=COS_EPS,
                in1=scnt[:, CH:2 * CH], op0=ALU.add, op1=ALU.add),
                 reads=[b("x1b")], writes=[b("x1b")])
            P.op("vector", lambda e: e.reciprocal(out=qkt[:, CH:2 * CH], in_=qkt[:, 0:CH]),
                 reads=[b("x1b")], writes=[b("x1b")])
            P.op("vector", lambda e: e.tensor_tensor(out=scnt[:, 0:CH], in0=cq[:],
                                                     in1=qkt[:, CH:2 * CH], op=ALU.mult),
                 reads=[b("cq"), b("x1b")], writes=[b("x1b")])
            P.op("vector", lambda e: e.tensor_tensor(out=scnt[:, CH:2 * CH], in0=sq[:],
                                                     in1=qkt[:, CH:2 * CH], op=ALU.mult),
                 reads=[b("sq"), b("x1b")], writes=[b("x1b")])
        def front_part2b(c):
            for bank, off, src, srcbuf in (
                    (0, 0, coskr[:], "coskr"), (0, CH, sinkr[:], "sinkr"),
                    (1, 0, scnt[:, 0:CH], "x1b"), (1, CH, scnt[:, CH:2 * CH], "x1b")):
                P.group("tensor",
                        [lambda e, bank=bank, off=off, src=src: e.matmul(
                            pb[bank][:, off:off + CH], sel[:], src,
                            start=True, stop=True)],
                        reads=[b("sel"), b(srcbuf)], writes=[b(f"pb{bank}")])
            for name, bank, off in [("coskb", 0, 0), ("sinkb", 0, CH),
                                    ("cosqb", 1, 0), ("sinqb", 1, CH)]:
                tgt = {"coskb": coskb, "sinkb": sinkb, "cosqb": cosqb, "sinqb": sinqb}[name]
                P.op("scalar", lambda e, t=tgt, bk=bank, o=off: e.copy(
                    out=t[:], in_=pb[bk][:, o:o + CH]),
                     reads=[b(f"pb{bank}")], writes=[b(name)])
        def middle_vm(c):
            lc = c % CHB
            for fc in range(KD):
                vb = fc % 2
                fns = []
                for k in range(KD):
                    wh_ap = wqkvh[:, k * 1056 + fc * 128:k * 1056 + (fc + 1) * 128]
                    wl_ap = wqkvl[:, k * 1056 + fc * 128:k * 1056 + (fc + 1) * 128]
                    fns.append(lambda e, o=pb[vb][:, 0:2 * CH], w=wh_ap, r=yhl2(y1T2, k),
                               st=(k == 0): e.matmul(o, w, r, start=st, stop=False))
                    fns.append(lambda e, o=pb[vb][:, 0:CH], w=wl_ap, r=yh2(y1T2, k),
                               sp=(k == KD - 1): e.matmul(o, w, r, start=False, stop=sp))
                P.group("tensor", fns,
                        reads=[b("wqkvh"), b("wqkvl"), b("y1T2")],
                        writes=[b(f"pb{vb}")])
                P.op("scalar", lambda e, vb=vb: e.copy(
                    out=t1b[:], in_=pb[vb][:, CH:2 * CH]),
                     reads=[b(f"pb{vb}")], writes=[b("t1b")])
                P.op("vector", lambda e, vb=vb: e.tensor_tensor(
                    out=t2b[:], in0=pb[vb][:, 0:CH], in1=t1b[:], op=ALU.add),
                     reads=[b(f"pb{vb}"), b("t1b")], writes=[b("t2b")])
                if lc == 0:
                    P.op("gpsimd", lambda e: e.tensor_tensor(
                        out=mcos[:], in0=t2b[:], in1=coskb[:], op=ALU.mult),
                         reads=[b("t2b"), b("coskb")], writes=[b("mcos")], sem="gp")
                    P.op("gpsimd", lambda e: e.tensor_tensor(
                        out=msin[:], in0=t2b[:], in1=sinkb[:], op=ALU.mult),
                         reads=[b("t2b"), b("sinkb")], writes=[b("msin")], sem="gp")
                else:
                    P.op("gpsimd", lambda e: e.tensor_tensor(
                        out=mcos[:, 1:CH], in0=t2b[:, 1:CH], in1=coskb[:, 1:CH],
                        op=ALU.mult),
                         reads=[b("t2b"), b("coskb")], writes=[b("mcos")], sem="gp")
                    P.op("vector", lambda e, fc=fc: e.scalar_tensor_tensor(
                        out=mcos[:, 0:1], in0=t2b[:, 0:1], scalar=coskb[:, 0:1],
                        in1=carc[:, fc:fc + 1], op0=ALU.mult, op1=ALU.add),
                         reads=[b("t2b"), b("coskb"), b("carc")], writes=[b("mcos")])
                    P.op("gpsimd", lambda e: e.tensor_tensor(
                        out=msin[:, 1:CH], in0=t2b[:, 1:CH], in1=sinkb[:, 1:CH],
                        op=ALU.mult),
                         reads=[b("t2b"), b("sinkb")], writes=[b("msin")], sem="gp")
                    P.op("vector", lambda e, fc=fc: e.scalar_tensor_tensor(
                        out=msin[:, 0:1], in0=t2b[:, 0:1], scalar=sinkb[:, 0:1],
                        in1=cars[:, fc:fc + 1], op0=ALU.mult, op1=ALU.add),
                         reads=[b("t2b"), b("sinkb"), b("cars")], writes=[b("msin")])
                P.op("vector", lambda e: e.tensor_tensor_scan(
                    out=kvc[:], data0=onesN[:], data1=mcos[:], initial=0.0,
                    op0=ALU.mult, op1=ALU.add),
                     reads=[b("onesN"), b("mcos")], writes=[b("kvc")])
                P.op("vector", lambda e: e.tensor_tensor_scan(
                    out=kvs[:], data0=onesN[:], data1=msin[:], initial=0.0,
                    op0=ALU.mult, op1=ALU.add),
                     reads=[b("onesN"), b("msin")], writes=[b("kvs")])
                if lc != CHB - 1:
                    P.op("vector", lambda e, fc=fc: e.tensor_copy(
                        out=carc[:, fc:fc + 1], in_=kvc[:, CH - 1:CH]),
                         reads=[b("kvc")], writes=[b("carc")])
                    P.op("vector", lambda e, fc=fc: e.tensor_copy(
                        out=cars[:, fc:fc + 1], in_=kvs[:, CH - 1:CH]),
                         reads=[b("kvs")], writes=[b("cars")])
                P.op("gpsimd", lambda e: e.tensor_tensor(out=t1b[:], in0=kvc[:],
                                                         in1=cosqb[:], op=ALU.mult),
                     reads=[b("kvc"), b("cosqb")], writes=[b("t1b")], sem="gp")
                P.op("gpsimd", lambda e: e.tensor_tensor(out=t2b[:], in0=kvs[:],
                                                         in1=sinqb[:], op=ALU.mult),
                     reads=[b("kvs"), b("sinqb")], writes=[b("t2b")], sem="gp")
                P.op("vector", lambda e: e.tensor_tensor(out=hdf[:], in0=t1b[:],
                                                         in1=t2b[:], op=ALU.add),
                     reads=[b("t1b"), b("t2b")], writes=[b("hdf")])
                P.op("scalar", lambda e, fc=fc: e.copy(
                    out=hh2[:, fc * CH:(fc + 1) * CH], in_=hdf[:]),
                     reads=[b("hdf")], writes=[b("hh2")])
                P.op("vector", lambda e, fc=fc: e.tensor_tensor(
                    out=hl2[:, fc * CH:(fc + 1) * CH], in0=hdf[:],
                    in1=hh2[:, fc * CH:(fc + 1) * CH], op=ALU.subtract),
                     reads=[b("hdf"), b("hh2")], writes=[b("hl2")])

        def wo_stage(c):
            fns = []
            for fc in range(KD):
                for tt in range(TILES):
                    sh = hh2[:, fc * CH + tt * 128:fc * CH + (tt + 1) * 128]
                    sl = hl2[:, fc * CH + tt * 128:fc * CH + (tt + 1) * 128]
                    st = (fc == 0)
                    sp = (fc == KD - 1)
                    for dh in range(2):
                        o = pb[2 * tt + dh][:, 0:512]
                        wh_ap = woth[:, fc * D + dh * 512:fc * D + (dh + 1) * 512]
                        wl_ap = wotl[:, fc * D + dh * 512:fc * D + (dh + 1) * 512]
                        fns.append(lambda e, o=o, sh=sh, w=wh_ap, st=st:
                                   e.matmul(o, sh, w, start=st, stop=False))
                        fns.append(lambda e, o=o, sh=sh, w=wl_ap:
                                   e.matmul(o, sh, w, start=False, stop=False))
                    for dh in range(2):
                        o = pb[2 * tt + dh][:, 0:512]
                        wh_ap = woth[:, fc * D + dh * 512:fc * D + (dh + 1) * 512]
                        fns.append(lambda e, o=o, sl=sl, w=wh_ap, sp=sp:
                                   e.matmul(o, sl, w, start=False, stop=sp))
            P.group("tensor", fns, reads=[b("hh2"), b("hl2"), b("woth"), b("wotl")],
                    writes=[b(f"pb{i}") for i in range(4)])

        def a_xa_stage(c):
            for tt in range(TILES):
                P.op("scalar", lambda e, tt=tt: e.activation(
                    out=ytmp[:], in_=xb[:, tt * D:(tt + 1) * D], func=AF.Identity,
                    scale=rct[:, tt:tt + 1], bias=nmt[:, tt:tt + 1]),
                     reads=[b("xb"), b("rct"), b("nmt")], writes=[b("ytmp")])
                for dh in range(2):
                    P.op("vector", lambda e, tt=tt, dh=dh: e.tensor_tensor(
                        out=ab[:, tt * D + dh * 512:tt * D + (dh + 1) * 512],
                        in0=ytmp[:, dh * 512:(dh + 1) * 512],
                        in1=pb[2 * tt + dh][:, 0:512], op=ALU.add),
                         reads=[b("ytmp"), b(f"pb{2 * tt + dh}")], writes=[b("ab")])
            for tt in range(TILES):
                P.op("vector", lambda e, tt=tt: e.tensor_tensor(
                    out=xqb[:, tt * D:(tt + 1) * D], in0=xb[:, tt * D:(tt + 1) * D],
                    in1=ab[:, tt * D:(tt + 1) * D], op=ALU.add),
                     reads=[b("xb"), b("ab")], writes=[b("xqb")])

        def ln2_stage(c):
            for tt in range(TILES):
                at = ab[:, tt * D:(tt + 1) * D]
                P.op("vector", lambda e, at=at: e.bn_stats(bnb[:, 0:6], at[:, 0:512]),
                     reads=[b("ab")], writes=[b("bnb")])
                P.op("vector", lambda e, at=at: e.bn_stats(bnb[:, 6:12], at[:, 512:1024]),
                     reads=[b("ab")], writes=[b("bnb")])
                P.op("vector", lambda e: e.bn_aggr(mv[:], bnb[:]),
                     reads=[b("bnb")], writes=[b("mv")])
                P.op("scalar", lambda e: e.activation(out=sdc[:], in_=mv[:, 1:2],
                                                      func=AF.Sqrt, bias=epsc[:]),
                     reads=[b("mv"), b("epsc")], writes=[b("sdc")])
                P.op("vector", lambda e: e.reciprocal(out=rc[:], in_=sdc[:]),
                     reads=[b("sdc")], writes=[b("rc")])
                P.op("vector", lambda e: e.tensor_scalar(out=nmc[:], in0=mv[:, 0:1],
                                                         scalar1=rc[:], scalar2=-1.0,
                                                         op0=ALU.mult, op1=ALU.mult),
                     reads=[b("mv"), b("rc")], writes=[b("nmc")])
                P.op("scalar", lambda e, at=at: e.activation(
                    out=ytmp[:], in_=at, func=AF.Identity, scale=rc[:], bias=nmc[:]),
                     reads=[b("ab"), b("rc"), b("nmc")], writes=[b("ytmp")])
                for h4 in range(2):
                    fns = [(lambda e, k=k, h4=h4: e.transpose(
                        pb[h4][:, (k % 4) * 128:(k % 4 + 1) * 128],
                        ytmp[:, k * 128:(k + 1) * 128], ident[:]))
                        for k in range(h4 * 4, h4 * 4 + 4)]
                    P.group("tensor", fns, reads=[b("ytmp"), b("ident")],
                            writes=[b(f"pb{h4}")])
                    for k in range(h4 * 4, h4 * 4 + 4):
                        P.op("scalar", lambda e, k=k, h4=h4: e.activation(
                            out=tscr[:, (k % 4) * 128:(k % 4 + 1) * 128],
                            in_=pb[h4][:, (k % 4) * 128:(k % 4 + 1) * 128],
                            func=AF.Identity, scale=ln2w[:, k:k + 1],
                            bias=ln2b[:, k:k + 1]),
                             reads=[b(f"pb{h4}"), b("ln2w"), b("ln2b")],
                             writes=[b("tscr")])
                    hdst = y2T2[:].rearrange("p (k t) -> p k t", k=KD)[
                        :, h4 * 4:(h4 + 1) * 4, tt * 128:(tt + 1) * 128]
                    ldst = y2T2[:].rearrange("p (k t) -> p k t", k=KD)[
                        :, h4 * 4:(h4 + 1) * 4, CH + tt * 128:CH + (tt + 1) * 128]
                    tsrc = tscr[:].rearrange("p (k t) -> p k t", k=4)
                    P.op("scalar", lambda e, hdst=hdst, tsrc=tsrc: e.copy(
                        out=hdst, in_=tsrc),
                         reads=[b("tscr")], writes=[b("y2T2")])
                    P.op("vector", lambda e, hdst=hdst, ldst=ldst, tsrc=tsrc:
                         e.tensor_tensor(out=ldst, in0=tsrc, in1=hdst, op=ALU.subtract),
                         reads=[b("tscr"), b("y2T2")], writes=[b("y2T2")])

        def mlp_stage(c):
            w1_load(0, 0)
            w2_load(0, 0)
            w1_load(1, 1)
            w2_load(1, 1)
            w2_pending = []
            for g in range(NMG):
                gs = g % 2
                gh3 = g % 3
                g4 = g % 4
                w1bank = 2 + gs
                fns = []
                for k in range(KD):
                    wh_ap = w1gc[gs][:, k * 2 * MG:k * 2 * MG + MG]
                    wl_ap = w1gc[gs][:, k * 2 * MG + MG:(k + 1) * 2 * MG]
                    fns.append(lambda e, o=pb[w1bank][:, 0:2 * CH], w=wh_ap,
                               r=yhl2(y2T2, k), st=(k == 0):
                               e.matmul(o, w, r, start=st, stop=False))
                    fns.append(lambda e, o=pb[w1bank][:, 0:CH], w=wl_ap,
                               r=yh2(y2T2, k), sp=(k == KD - 1):
                               e.matmul(o, w, r, start=False, stop=sp))
                P.group("tensor", fns,
                        reads=[b(f"w1gc{gs}"), b("y2T2")],
                        writes=[b(f"pb{w1bank}")])
                P.op("scalar", lambda e, w1bank=w1bank: e.copy(
                    out=mcos[:], in_=pb[w1bank][:, CH:2 * CH]),
                     reads=[b(f"pb{w1bank}")], writes=[b("mcos")])
                P.op("vector", lambda e, w1bank=w1bank: e.tensor_tensor(
                    out=msin[:], in0=pb[w1bank][:, 0:CH], in1=mcos[:], op=ALU.add),
                     reads=[b(f"pb{w1bank}"), b("mcos")], writes=[b("msin")])
                P.op("scalar", lambda e, g=g:
                     e.activation(out=hgf[:], in_=msin[:],
                                  func=AF.Gelu_apprx_tanh, bias=b1c[:, g:g + 1]),
                     reads=[b("msin"), b("b1c")], writes=[b("hgf")])
                P.op("scalar", lambda e, gh3=gh3: e.copy(out=hgh[gh3][:], in_=hgf[:]),
                     reads=[b("hgf")], writes=[b(f"hgh{gh3}")])
                P.op("vector", lambda e, gh3=gh3: e.tensor_tensor(
                    out=hgl[gh3][:], in0=hgf[:], in1=hgh[gh3][:], op=ALU.subtract),
                     reads=[b("hgf"), b(f"hgh{gh3}")], writes=[b(f"hgl{gh3}")])
                fns = []
                for tt in range(TILES):
                    sh = hgh[gh3][:, tt * 128:(tt + 1) * 128]
                    sl = hgl[gh3][:, tt * 128:(tt + 1) * 128]
                    st = (g == 0)
                    for dh in range(2):
                        o = pb[4 + 2 * tt + dh][:, 0:512]
                        wh_ap = w2gc[g4][:, dh * 512:(dh + 1) * 512]
                        wl_ap = w2gc[g4][:, D + dh * 512:D + (dh + 1) * 512]
                        fns.append(lambda e, o=o, sh=sh, w=wh_ap, st=st:
                                   e.matmul(o, sh, w, start=st, stop=False))
                        fns.append(lambda e, o=o, sh=sh, w=wl_ap:
                                   e.matmul(o, sh, w, start=False, stop=False))
                    for dh in range(2):
                        o = pb[4 + 2 * tt + dh][:, 0:512]
                        wh_ap = w2gc[g4][:, dh * 512:(dh + 1) * 512]
                        fns.append(lambda e, o=o, sl=sl, w=wh_ap:
                                   e.matmul(o, sl, w, start=False, stop=False))
                wrt = [b(f"pb{4 + i}") for i in range(4)] if g == 0 else []
                w2_pending.append((fns,
                                   [b(f"hgh{gh3}"), b(f"hgl{gh3}"),
                                    b(f"w2gc{g4}")], wrt))
                if g >= 2:
                    fns2, rds2, wrt2 = w2_pending.pop(0)
                    P.group("tensor", fns2, reads=rds2, writes=wrt2)
                if g + 2 < NMG:
                    w1_load(g + 2, gs)
                    w2_load(g + 2, (g + 2) % 4)
                if g == 1 and c + 1 < NCHUNK:
                    front_part1(c + 1)
                if g == 8 and c + 1 < NCHUNK:
                    front_part2(c + 1)
                if g == 12 and c + 1 < NCHUNK:
                    front_part2b(c + 1)
                if g == 18 and c + 1 < NCHUNK:
                    middle_vm(c + 1)
            while w2_pending:
                fns2, rds2, wrt2 = w2_pending.pop(0)
                P.group("tensor", fns2, reads=rds2, writes=wrt2)
            fns = []
            for tt in range(TILES):
                for dh in range(2):
                    fns.append(lambda e, tt=tt, dh=dh: e.matmul(
                        pb[4 + 2 * tt + dh][:, 0:512], onesrow[:, 0:128],
                        b2row[:, dh * 512:(dh + 1) * 512], start=False, stop=True))
            P.group("tensor", fns, reads=[b("onesrow"), b("b2row")],
                    writes=[b(f"pb{4 + i}") for i in range(4)])

        def x1_stage(c):
            for tt in range(TILES):
                for dh in range(2):
                    P.op("vector", lambda e, tt=tt, dh=dh: e.tensor_tensor(
                        out=ab[:, tt * D + dh * 512:tt * D + (dh + 1) * 512],
                        in0=xqb[:, tt * D + dh * 512:tt * D + (dh + 1) * 512],
                        in1=pb[4 + 2 * tt + dh][:, 0:512], op=ALU.add),
                         reads=[b("xqb"), b(f"pb{4 + 2 * tt + dh}")], writes=[b("ab")])

        def ln3_quant_tile(c, tt):
            if True:
                x1t = ab[:, tt * D:(tt + 1) * D]
                P.op("vector", lambda e, x1t=x1t: e.bn_stats(bnb[:, 0:6], x1t[:, 0:512]),
                     reads=[b("ab")], writes=[b("bnb")])
                P.op("vector", lambda e, x1t=x1t: e.bn_stats(bnb[:, 6:12], x1t[:, 512:1024]),
                     reads=[b("ab")], writes=[b("bnb")])
                P.op("vector", lambda e: e.bn_aggr(mv[:], bnb[:]),
                     reads=[b("bnb")], writes=[b("mv")])
                P.op("scalar", lambda e: e.activation(out=sdc[:], in_=mv[:, 1:2],
                                                      func=AF.Sqrt, bias=epsc[:]),
                     reads=[b("mv"), b("epsc")], writes=[b("sdc")])
                P.op("vector", lambda e: e.reciprocal(out=rc[:], in_=sdc[:]),
                     reads=[b("sdc")], writes=[b("rc")])
                P.op("vector", lambda e: e.tensor_scalar(out=nmc[:], in0=mv[:, 0:1],
                                                         scalar1=rc[:], scalar2=-1.0,
                                                         op0=ALU.mult, op1=ALU.mult),
                     reads=[b("mv"), b("rc")], writes=[b("nmc")])
                P.op("scalar", lambda e, x1t=x1t: e.activation(
                    out=ytmp[:], in_=x1t, func=AF.Identity, scale=rc[:], bias=nmc[:]),
                     reads=[b("ab"), b("rc"), b("nmc")], writes=[b("ytmp")])
                P.op("vector", lambda e: e.tensor_reduce(
                    out=amaxc[:], in_=ytmp[:], axis=AX.X, op=ALU.max,
                    apply_absolute_value=True),
                     reads=[b("ytmp")], writes=[b("amaxc")])
                P.op("vector", lambda e: e.tensor_scalar(
                    out=mxc[:], in0=amaxc[:], scalar1=1e-5, scalar2=None, op0=ALU.max),
                     reads=[b("amaxc")], writes=[b("mxc")])
                P.op("vector", lambda e: e.reciprocal(out=rcpmc[:], in_=mxc[:]),
                     reads=[b("mxc")], writes=[b("rcpmc")])
                P.op("vector", lambda e, tt=tt: e.tensor_scalar(
                    out=rsc[tt][:], in0=mxc[:], scalar1=fscale[:], scalar2=1.0 / 127.0,
                    op0=ALU.mult, op1=ALU.mult),
                     reads=[b("mxc"), b("fscale")], writes=[b(f"rsc{tt}")])
                P.op("vector", lambda e: e.tensor_scalar(
                    out=tmpA[:], in0=ytmp[:], scalar1=rcpmc[:], scalar2=127.0,
                    op0=ALU.mult, op1=ALU.mult),
                     reads=[b("ytmp"), b("rcpmc")], writes=[b("tmpA")])
                P.op("vector", lambda e: e.tensor_copy(out=p2h[:], in_=tmpA[:]),
                     reads=[b("tmpA")], writes=[b("p2h")])
                P.op("vector", lambda e: e.tensor_tensor(
                    out=ytmp[:], in0=tmpA[:], in1=p2h[:], op=ALU.subtract),
                     reads=[b("tmpA"), b("p2h")], writes=[b("ytmp")])
                P.op("vector", lambda e: e.scalar_tensor_tensor(
                    out=tmpA[:], in0=ytmp[:], scalar=0.5, in1=p2h[:],
                    op0=ALU.is_gt, op1=ALU.add),
                     reads=[b("ytmp"), b("p2h")], writes=[b("tmpA")])
                P.op("vector", lambda e: e.tensor_scalar(
                    out=p2h[:], in0=ytmp[:], scalar1=-0.5,
                    scalar2=None, op0=ALU.is_lt),
                     reads=[b("ytmp")], writes=[b("p2h")])
                P.op("vector", lambda e, tt=tt: e.tensor_tensor(
                    out=xqb[:, tt * D:(tt + 1) * D], in0=tmpA[:], in1=p2h[:],
                    op=ALU.subtract),
                     reads=[b("tmpA"), b("p2h")], writes=[b("xqb")])

        def xq_trans_stage(c):
            for tt in range(TILES):
                for h4 in range(2):
                    tb = 4 + h4
                    fns = [(lambda e, k=k, tb=tb, tt=tt: e.transpose(
                        pb[tb][:, (k % 4) * 128:(k % 4 + 1) * 128],
                        xqb[:, tt * D + k * 128:tt * D + (k + 1) * 128], ident[:]))
                        for k in range(h4 * 4, h4 * 4 + 4)]
                    P.group("tensor", fns, reads=[b("xqb"), b("ident")],
                            writes=[b(f"pb{tb}")])
                    for k in range(h4 * 4, h4 * 4 + 4):
                        P.op("scalar", lambda e, k=k, tb=tb, tt=tt: e.copy(
                            out=xqT[:, k * CH + tt * 128:k * CH + (tt + 1) * 128],
                            in_=pb[tb][:, (k % 4) * 128:(k % 4 + 1) * 128]),
                             reads=[b(f"pb{tb}")], writes=[b("xqT")])

        def x1f_stage(c):
            for tt in range(TILES):
                P.op("vector", lambda e, tt=tt: e.tensor_tensor(
                    out=x1b[:, tt * D:(tt + 1) * D],
                    in0=ab[:, tt * D:(tt + 1) * D], in1=fbias[:], op=ALU.add),
                     reads=[b("ab"), b("fbias")], writes=[b("x1b")])

        def ffn_out_stage(c):
            ffnbank = {0: (6, 7), 1: (4, 5)}
            stagebuf = {0: (tmpA, "tmpA"), 1: (ytmp, "ytmp")}
            for tt in range(TILES):
                fns = []
                for dh in range(2):
                    for k in range(KD):
                        fns.append(lambda e, o=pb[ffnbank[tt][dh]][:, 0:512],
                                   tt=tt, dh=dh, k=k: e.matmul(
                            o,
                            xqT[:, k * CH + tt * 128:k * CH + (tt + 1) * 128],
                            ternT[:, k * D + dh * 512:k * D + (dh + 1) * 512],
                            start=(k == 0), stop=(k == KD - 1)))
                P.group("tensor", fns, reads=[b("xqT"), b("ternT")],
                        writes=[b(f"pb{ffnbank[tt][0]}"), b(f"pb{ffnbank[tt][1]}")])
                sbuf_t, sname = stagebuf[tt]
                for dh in range(2):
                    P.op("vector", lambda e, tt=tt, dh=dh, sbuf_t=sbuf_t:
                         e.scalar_tensor_tensor(
                        out=sbuf_t[:, dh * 512:(dh + 1) * 512],
                        in0=pb[ffnbank[tt][dh]][:, 0:512], scalar=rsc[tt][:],
                        in1=x1b[:, tt * D + dh * 512:tt * D + (dh + 1) * 512],
                        op0=ALU.mult, op1=ALU.add),
                         reads=[b(f"pb{ffnbank[tt][dh]}"), b(f"rsc{tt}"), b("x1b")],
                         writes=[b(sname)])
                dst = out_d.rearrange("(c t p) d -> c p t d",
                                      c=NCHUNK, t=TILES)[c, :, tt, :]
                dma("gpsimd", dst, sbuf_t[:], reads=[b(sname)],
                    sem="qo" if tt == 0 else "qo2")

        front_part1(0)
        front_part2(0)
        front_part2b(0)
        middle_vm(0)
        wo_stage(0)
        a_xa_stage(0)
        x_load(1)
        for c in range(NCHUNK):
            ln2_stage(c)
            mlp_stage(c)
            x1_stage(c)
            if c + 1 < NCHUNK:
                wo_stage(c + 1)
            ln3_quant_tile(c, 0)
            ln3_quant_tile(c, 1)
            x1f_stage(c)
            xq_trans_stage(c)
            if c + 1 < NCHUNK:
                a_xa_stage(c + 1)
                if c + 2 < NCHUNK:
                    x_load(c + 2)
            ffn_out_stage(c)

        P.items["gpsimd"].append(("w", "qo", P.cnt["qo"]))
        P.items["gpsimd"].append(("w", "qo2", P.cnt["qo2"]))

        with nc.Block() as block:
            def replay(eng_obj, name):
                for it in P.items[name]:
                    if it[0] == "w":
                        eng_obj.wait_ge(sems[it[1]], it[2])
                    else:
                        instr = it[1](eng_obj)
                        if it[2] is not None:
                            instr.then_inc(sems[it[2]], it[3])

            @block.sync
            def _(sync):
                replay(sync, "sync")

            @block.tensor
            def _(tensor):
                replay(tensor, "tensor")

            @block.vector
            def _(vector):
                replay(vector, "vector")

            @block.scalar
            def _(scalar):
                replay(scalar, "scalar")

            @block.gpsimd
            def _(gpsimd):
                replay(gpsimd, "gpsimd")

    return nc


def kernel(**inputs):
    in_maps = _prep_inputs(inputs)
    nc = build_program()
    res = run_bass_kernel_spmd(nc, in_maps, core_ids=list(range(NCORES)))
    out = np.concatenate(
        [np.asarray(r["out"]).reshape(B_PER_CORE, T, D) for r in res.results], axis=0)
    return out


# version 17
# speedup vs baseline: 1.0660x; 1.0660x over previous
import numpy as np
import ml_dtypes
from contextlib import ExitStack

import concourse.bass as bass
import concourse.mybir as mybir
from concourse.bass_utils import run_bass_kernel_spmd

F32 = mybir.dt.float32
FP8 = mybir.dt.float8e4
BF16 = mybir.dt.bfloat16
I32 = mybir.dt.int32
AF = mybir.ActivationFunctionType
ALU = mybir.AluOpType
AX = mybir.AxisListType

PI = float(np.pi)
LN_EPS = 1e-5
COS_EPS = 1e-6

NCORES = 8
B = 16
T = 4096
D = 1024
KD = D // 128
NH = 16
DH = 64
DM = 4096
B_PER_CORE = B // NCORES
NTOK = B_PER_CORE * T
CH = 256
TILES = CH // 128
NCHUNK = NTOK // CH
CHB = T // CH
MG = 128
NMG = DM // MG

ENG_SEM = {"tensor": "pe", "vector": "dve", "scalar": "act"}


class Buf:
    __slots__ = ("name", "ready", "readers", "war")

    def __init__(self, name, war=True):
        self.name = name
        self.ready = None
        self.readers = []
        self.war = war


class Prog:
    ENGS = ("sync", "tensor", "vector", "scalar", "gpsimd")

    def __init__(self):
        self.items = {e: [] for e in self.ENGS}
        self.cnt = {}
        self.marks = {}

    def _wait(self, eng, cond):
        s, v = cond
        if self.marks.get((eng, s), 0) >= v:
            return
        self.marks[(eng, s)] = v
        self.items[eng].append(("w", s, v))

    def _ready_conds(self, b):
        if b.ready is None:
            return ()
        if isinstance(b.ready, list):
            return b.ready
        return (b.ready,)

    def _prewait(self, eng, reads, writes):
        for b in reads:
            for c in self._ready_conds(b):
                self._wait(eng, c)
        for b in writes:
            for c in self._ready_conds(b):
                self._wait(eng, c)
            if b.war:
                for c in b.readers:
                    self._wait(eng, c)

    def _post(self, cond, reads, writes):
        for b in writes:
            b.ready = cond
            b.readers = []
        for b in reads:
            if b.war:
                b.readers.append(cond)

    def op(self, eng, fn, reads=(), writes=(), sem=None, k=1):
        if sem is None:
            sem = ENG_SEM[eng]
        self._prewait(eng, reads, writes)
        self.cnt[sem] = self.cnt.get(sem, 0) + k
        cond = (sem, self.cnt[sem])
        self.items[eng].append(("o", fn, sem, k))
        self._post(cond, reads, writes)
        return cond

    def group(self, eng, fns, reads=(), writes=(), sem=None):
        if sem is None:
            sem = ENG_SEM[eng]
        self._prewait(eng, reads, writes)
        for fn in fns[:-1]:
            self.items[eng].append(("o", fn, None, 0))
        self.cnt[sem] = self.cnt.get(sem, 0) + 1
        cond = (sem, self.cnt[sem])
        self.items[eng].append(("o", fns[-1], sem, 1))
        self._post(cond, reads, writes)
        return cond


def _prep_inputs(inputs):
    f32 = np.float32
    W_Q = np.asarray(inputs["L0_attn_W_Q"], f32)
    W_K = np.asarray(inputs["L0_attn_W_K"], f32)
    W_V = np.asarray(inputs["L0_attn_W_V"], f32)
    W_O = np.asarray(inputs["L0_attn_W_O"], f32)
    wqkv = np.ascontiguousarray(np.concatenate(
        [W_V.transpose(1, 2, 0).reshape(D, NH * DH),
         W_Q[:, :, 0].T, W_K[:, :, 0].T], axis=1))
    wot = np.ascontiguousarray(W_O.reshape(D, DH * NH).T)
    w1 = np.ascontiguousarray(np.asarray(inputs["L0_attn_W1"], f32))
    w2 = np.ascontiguousarray(np.asarray(inputs["L0_attn_W2"], f32))
    ln1w_c = np.ascontiguousarray(np.asarray(inputs["L0_attn_ln1_w"], f32).reshape(KD, 128).T)
    ln1b_c = np.ascontiguousarray(np.asarray(inputs["L0_attn_ln1_b"], f32).reshape(KD, 128).T)
    ln2w_c = np.ascontiguousarray(np.asarray(inputs["L0_attn_ln2_w"], f32).reshape(KD, 128).T)
    ln2b_c = np.ascontiguousarray(np.asarray(inputs["L0_attn_ln2_b"], f32).reshape(KD, 128).T)
    b1c = np.ascontiguousarray(np.asarray(inputs["L0_attn_b1"], f32).reshape(DM // 128, 128).T)
    b2row = np.ascontiguousarray(np.asarray(inputs["L0_attn_b2"], f32).reshape(1, D))
    fbias = np.ascontiguousarray(
        np.broadcast_to(np.asarray(inputs["L0_ffn_bias"], f32), (128, D)))
    fscale = np.ascontiguousarray(
        np.broadcast_to(np.asarray(inputs["L0_ffn_weight_scale"], f32).reshape(1, 1), (128, 1)))
    packed = np.ascontiguousarray(np.asarray(inputs["L0_ffn_packed_weight"], np.int32).reshape(256, 1024))
    ident = np.eye(128, dtype=f32)
    sel = np.ascontiguousarray(np.tile(np.eye(NH, dtype=f32), (1, 128 // NH)))
    common = dict(wqkv=wqkv, wot=wot, w1=w1, w2=w2, ln1w_c=ln1w_c, ln1b_c=ln1b_c,
                  ln2w_c=ln2w_c, ln2b_c=ln2b_c, b1c=b1c, b2row=b2row, fbias=fbias,
                  fscale=fscale, packed=packed, ident=ident, sel=sel)
    x_full = np.asarray(inputs["L0_attn_x"], f32)
    in_maps = []
    for j in range(NCORES):
        m = dict(common)
        m["x"] = np.ascontiguousarray(
            x_full[j * B_PER_CORE:(j + 1) * B_PER_CORE].reshape(NTOK, D))
        in_maps.append(m)
    return in_maps


def build_program():
    nc = bass.Bass("TRN2", target_bir_lowering=False)
    P = Prog()

    dr = {}
    for name, shape, dt in [
        ("x", [NTOK, D], F32), ("wqkv", [D, 1056], F32), ("wot", [D, D], F32),
        ("w1", [D, DM], F32), ("w2", [DM, D], F32),
        ("ln1w_c", [128, KD], F32), ("ln1b_c", [128, KD], F32),
        ("ln2w_c", [128, KD], F32), ("ln2b_c", [128, KD], F32),
        ("b1c", [128, DM // 128], F32), ("b2row", [1, D], F32),
        ("fbias", [128, D], F32), ("fscale", [128, 1], F32),
        ("packed", [256, 1024], I32), ("ident", [128, 128], F32),
        ("sel", [NH, 128], F32),
    ]:
        dr[name] = nc.dram_tensor(name, shape, dt, kind="ExternalInput").ap()
    out_d = nc.dram_tensor("out", [NTOK, D], F32, kind="ExternalOutput").ap()
    scr = {}
    for name, shape in [("w1s", [D, 2 * DM]), ("w2s", [DM, 2 * D])]:
        scr[name] = nc.dram_tensor(name, shape, BF16, kind="Internal").ap()

    stack = ExitStack()

    def sb(name, shape, dt=F32):
        return stack.enter_context(nc.sbuf_tensor(name, shape, dt))

    with stack:
        ident = sb("identS", [128, 128])
        sel = sb("selS", [NH, 128])
        wqkvh = sb("wqkvhS", [128, KD * 1056], BF16)
        wqkvl = sb("wqkvlS", [128, KD * 1056], BF16)
        woth = sb("wothS", [128, KD * D], BF16)
        wotl = sb("wotlS", [128, KD * D], BF16)
        ln1w = sb("ln1wS", [128, KD]); ln1b = sb("ln1bS", [128, KD])
        ln2w = sb("ln2wS", [128, KD]); ln2b = sb("ln2bS", [128, KD])
        b1c = sb("b1cS", [128, DM // 128])
        b2row = sb("b2rowS", [1, D])
        fbias = sb("fbiasS", [128, D])
        fscale = sb("fscaleS", [128, 1])
        onesrow = sb("onesrowS", [1, 128])
        ternT = sb("ternTS", [128, KD * D], FP8)
        p2h = sb("p2hS", [128, 1024], I32)
        onesN = sb("onesNS", [128, CH])
        epsc = sb("epscS", [128, 1])
        eps2c = sb("eps2cS", [128, 1])
        halfpi = sb("halfpiS", [128, 1])
        xb = sb("xbS", [128, TILES * D])
        w1gc = [sb(f"w1gcS{i}", [128, KD * 2 * MG], BF16) for i in range(2)]
        w2gc = [sb(f"w2gcS{i}", [128, 2 * D], BF16) for i in range(4)]
        ab = sb("abS", [128, TILES * D])
        x1b = sb("x1bS", [128, TILES * D])
        y1T2 = sb("y1T2S", [128, KD * 2 * CH], BF16)
        y2T2 = sb("y2T2S", [128, KD * 2 * CH], BF16)
        tscr = sb("tscrS", [128, 512])
        ytmp = sb("ytmpS", [128, D])
        tmpA = sb("tmpAS", [128, D])
        hgh = [sb(f"hghS{i}", [128, CH], BF16) for i in range(3)]
        hgl = [sb(f"hglS{i}", [128, CH], BF16) for i in range(3)]
        hgf = sb("hgfS", [128, CH])
        hh2 = sb("hh2S", [128, KD * CH], BF16)
        hl2 = sb("hl2S", [128, KD * CH], BF16)
        hdf = sb("hdfS", [128, CH])
        kvc = sb("kvcS", [128, CH]); kvs = sb("kvsS", [128, CH])
        mcos = sb("mcosS", [128, CH]); msin = sb("msinS", [128, CH])
        t1b = sb("t1bS", [128, CH]); t2b = sb("t2bS", [128, CH])
        coskb = sb("coskbS", [128, CH]); sinkb = sb("sinkbS", [128, CH])
        cosqb = sb("cosqbS", [128, CH]); sinqb = sb("sinqbS", [128, CH])
        carc = sb("carcS", [128, KD]); cars = sb("carsS", [128, KD])
        xqb = sb("xqbS", [128, TILES * D])
        xqT = sb("xqTS", [128, KD * CH], BF16)
        i32s = sb("i32sS", [128, 512], I32)
        coskr = sb("coskrS", [NH, CH]); sinkr = sb("sinkrS", [NH, CH])
        cq = sb("cqS", [NH, CH]); sq = sb("sqS", [NH, CH])
        ckb = sb("ckbS", [NH, 2 * CH])
        carry2 = sb("carry2S", [NH, 2])
        bnb = sb("bnbS", [128, 12])
        mv = sb("mvS", [128, 2])
        fbnb = sb("fbnbS", [128, 12])
        fmv = sb("fmvS", [128, 2])
        fs1c = sb("fs1cS", [128, 1]); fsdc = sb("fsdcS", [128, 1])
        fr1t = sb("fr1tS", [128, TILES]); fv1t = sb("fv1tS", [128, TILES])
        rct = sb("rctS", [128, TILES]); nmt = sb("nmtS", [128, TILES])
        sdc = sb("sdcS", [128, 1]); rc = sb("rcS", [128, 1])
        nmc = sb("nmcS", [128, 1]); v1c = sb("v1cS", [128, 1])
        s1c = sb("s1cS", [128, 1]); r1c = sb("r1cS", [128, 1])
        amaxc = sb("amaxcS", [128, 1]); mxc = sb("mxcS", [128, 1])
        rcpmc = sb("rcpmcS", [128, 1]); ascc = sb("asccS", [128, 1])
        rsc = [sb(f"rscS{i}", [128, 1]) for i in range(TILES)]
        qkt = x1b[0:NH, 0:2 * CH]
        scnt = x1b[0:NH, 2 * CH:4 * CH]

        pb = [stack.enter_context(nc.psum_tensor(f"pb{i}", [128, 512], F32))
              for i in range(8)]

        sems = {n: stack.enter_context(nc.semaphore(n))
                for n in ("pe", "dve", "act", "gp", "qc", "qp", "qx", "qx2",
                          "qsp", "qsl", "qsp2", "qsl2",
                          "w1ah", "w1al", "w1bh", "w1bl",
                          "w2ah", "w2al", "w2bh", "w2bl", "w2ch", "w2cl",
                          "w2dh", "w2dl", "qo", "qo2")}

        bufs = {}

        def BF(name, war=True):
            bufs[name] = Buf(name, war)
            return bufs[name]

        const_names = ["ident", "sel", "ln1w", "ln1b",
                       "ln2w", "ln2b", "b1c", "b2row", "fbias", "fscale",
                       "onesrow", "onesN", "epsc", "eps2c", "halfpi",
                       "w1hD", "w1lD", "w2hD", "w2lD"]
        for n in const_names:
            BF(n, war=False)
        for n in ["wqkvh", "wqkvl", "woth", "wotl", "tscr", "hgf", "hdf",
                  "ternT", "p2h", "xb", "ab", "x1b", "y1T2", "y2T2", "ytmp",
                  "fbnb", "fmv", "fs1c", "fsdc", "fr1t", "fv1t", "rct", "nmt",
                  "tmpA", "kvc", "kvs", "mcos", "msin", "t1b", "t2b",
                  "coskb", "sinkb", "cosqb", "sinqb", "carc", "cars",
                  "xqb", "xqT", "i32s", "coskr", "sinkr", "cq", "sq",
                  "ckb", "carry2",
                  "bnb", "mv", "sdc", "rc", "nmc", "v1c", "s1c", "r1c",
                  "amaxc", "mxc", "rcpmc", "ascc"]:
            BF(n)
        BF("hh2"); BF("hl2")
        for i in range(2):
            BF(f"w1gc{i}")
        for i in range(3):
            BF(f"hgh{i}"); BF(f"hgl{i}")
        for i in range(4):
            BF(f"w2gc{i}")
        for i in range(TILES):
            BF(f"rsc{i}")
        for i in range(8):
            BF(f"pb{i}")

        def b(n):
            return bufs[n]

        def dma(eng, dst_ap, src_ap, reads=(), writes=(), sem=None):
            P.op(eng, lambda e, d=dst_ap, s=src_ap: e.dma_start(out=d, in_=s),
                 reads=reads, writes=writes, sem=sem, k=16)

        qg = dict(sem="qc")
        const_bufs = []
        dma("gpsimd", ident[:], dr["ident"], writes=[b("ident")], **qg)
        dma("gpsimd", sel[:], dr["sel"], writes=[b("sel")], **qg)
        for n in ["ln1w_c", "ln1b_c", "ln2w_c", "ln2b_c"]:
            tgt = {"ln1w_c": ln1w, "ln1b_c": ln1b, "ln2w_c": ln2w, "ln2b_c": ln2b}[n]
            dma("gpsimd", tgt[:], dr[n], writes=[b(n[:4])], **qg)
        dma("gpsimd", b1c[:], dr["b1c"], writes=[b("b1c")], **qg)
        dma("gpsimd", b2row[:], dr["b2row"], writes=[b("b2row")], **qg)
        dma("gpsimd", fbias[:], dr["fbias"], writes=[b("fbias")], **qg)
        dma("gpsimd", fscale[:], dr["fscale"], writes=[b("fscale")], **qg)
        for n in ["ident", "sel", "ln1w", "ln1b",
                  "ln2w", "ln2b", "b1c", "b2row", "fbias", "fscale"]:
            bufs[n].ready = ("qc", P.cnt["qc"])

        def split_block(src_dram_ap, nfree, hi_ap, lo_ap, hi_buf, lo_buf,
                        out_h=None, out_l=None):
            dma("sync", xb[:, 0:nfree], src_dram_ap, writes=[b("xb")], sem="qx")
            P.op("scalar", lambda e, hi_ap=hi_ap, nfree=nfree: e.copy(
                out=hi_ap, in_=xb[:, 0:nfree]),
                 reads=[b("xb")], writes=[hi_buf])
            P.op("vector", lambda e, hi_ap=hi_ap, lo_ap=lo_ap, nfree=nfree:
                 e.tensor_tensor(out=lo_ap, in0=xb[:, 0:nfree], in1=hi_ap,
                                 op=ALU.subtract),
                 reads=[b("xb"), hi_buf], writes=[lo_buf])
            if out_h is not None:
                dma("gpsimd", out_h, hi_ap, reads=[hi_buf], sem="qsp")
                dma("gpsimd", out_l, lo_ap, reads=[lo_buf], sem="qsl")

        for k in range(KD):
            split_block(dr["wqkv"][k * 128:(k + 1) * 128, :], 1056,
                        wqkvh[:, k * 1056:(k + 1) * 1056],
                        wqkvl[:, k * 1056:(k + 1) * 1056],
                        b("wqkvh"), b("wqkvl"))
        spx = [Buf("spx0"), Buf("spx1")]
        sph = [Buf("sph0"), Buf("sph1")]
        spl = [Buf("spl0"), Buf("spl1")]
        for bb_ in spx:
            bb_.ready = bufs["xb"].ready
            bb_.readers = list(bufs["xb"].readers)
        for bb_ in sph:
            bb_.ready = bufs["xqT"].ready
        for bb_ in spl:
            bb_.ready = bufs["y1T2"].ready
        blocks = []
        for k in range(KD):
            blocks.append(("wot", dr["wot"][k * 128:(k + 1) * 128, :],
                           woth[:, k * D:(k + 1) * D], wotl[:, k * D:(k + 1) * D],
                           None, None))
        for k in range(KD):
            for hf in range(4):
                base = scr["w1s"][k * 128:(k + 1) * 128, hf * 2 * D:(hf + 1) * 2 * D]
                hi_dst = base.rearrange("p (g m) -> p g m", g=8)[:, :, 0:128]
                lo_dst = base.rearrange("p (g m) -> p g m", g=8)[:, :, 128:256]
                blocks.append(("w1",
                               dr["w1"][k * 128:(k + 1) * 128, hf * D:(hf + 1) * D],
                               None, None, hi_dst, lo_dst))
        for mk in range(DM // 128):
            blocks.append(("w2", dr["w2"][mk * 128:(mk + 1) * 128, :],
                           None, None,
                           scr["w2s"][mk * 128:(mk + 1) * 128, 0:D],
                           scr["w2s"][mk * 128:(mk + 1) * 128, D:2 * D]))
        for i, (kind, src, hdst, ldst, oh, ol) in enumerate(blocks):
            sl = i % 2
            xh_ap = xb[:, sl * D:(sl + 1) * D]
            hi_ap = hdst if hdst is not None else xqT[:, sl * D:(sl + 1) * D]
            lo_ap = ldst if ldst is not None else y1T2[:, sl * D:(sl + 1) * D]
            hi_buf = b("woth") if hdst is not None else sph[sl]
            lo_buf = b("wotl") if ldst is not None else spl[sl]
            dma("sync", xh_ap, src, writes=[spx[sl]], sem="qx" if sl == 0 else "qx2")
            P.op("scalar", lambda e, hi_ap=hi_ap, xh_ap=xh_ap: e.copy(
                out=hi_ap, in_=xh_ap),
                 reads=[spx[sl]], writes=[hi_buf])
            P.op("vector", lambda e, hi_ap=hi_ap, lo_ap=lo_ap, xh_ap=xh_ap:
                 e.tensor_tensor(out=lo_ap, in0=xh_ap, in1=hi_ap, op=ALU.subtract),
                 reads=[spx[sl], hi_buf], writes=[lo_buf])
            if oh is not None:
                hi_src = hi_ap.rearrange("p (g m) -> p g m", g=8) \
                    if kind == "w1" else hi_ap
                lo_src = lo_ap.rearrange("p (g m) -> p g m", g=8) \
                    if kind == "w1" else lo_ap
                dma("gpsimd", oh, hi_src, reads=[hi_buf],
                    sem="qsp" if sl == 0 else "qsp2")
                dma("gpsimd", ol, lo_src, reads=[lo_buf],
                    sem="qsl" if sl == 0 else "qsl2")
        for tgt, parts in (("xb", spx), ("xqT", sph), ("y1T2", spl)):
            rd = []
            for bb_ in parts:
                if bb_.ready is not None:
                    rd.append(bb_.ready)
                rd += bb_.readers
            bufs[tgt].readers += rd
        for n in ["w1hD", "w2hD"]:
            bufs[n].ready = [("qsp", P.cnt.get("qsp", 0)),
                             ("qsp2", P.cnt.get("qsp2", 0)),
                             ("qsl", P.cnt.get("qsl", 0)),
                             ("qsl2", P.cnt.get("qsl2", 0))]

        P.op("vector", lambda e: e.memset(onesN[:], 1.0), writes=[b("onesN")])
        P.op("vector", lambda e: e.memset(epsc[:], LN_EPS), writes=[b("epsc")])
        P.op("vector", lambda e: e.memset(eps2c[:], LN_EPS * LN_EPS), writes=[b("eps2c")])
        P.op("vector", lambda e: e.memset(halfpi[:], PI / 2), writes=[b("halfpi")])
        P.op("vector", lambda e: e.memset(onesrow[:], 1.0), writes=[b("onesrow")])

        for half in range(2):
            dma("gpsimd", p2h[:], dr["packed"][half * 128:(half + 1) * 128, :],
                writes=[b("p2h")], sem="qp")
            for oc2 in range(4):
                oc = oc2 * 2 + half
                s = 2 * (oc // 2)
                toc = xqb[:, 0:D]
                for hv in range(2):
                    P.op("vector",
                         lambda e, s=s, hv=hv: e.tensor_scalar(
                             out=i32s[:], in0=p2h[:, hv * 512:(hv + 1) * 512],
                             scalar1=s, scalar2=3,
                             op0=ALU.logical_shift_right, op1=ALU.bitwise_and),
                         reads=[b("p2h")], writes=[b("i32s")])
                    P.op("vector",
                         lambda e, toc=toc, hv=hv: e.tensor_scalar(
                             out=toc[:, hv * 512:(hv + 1) * 512], in0=i32s[:],
                             scalar1=-1.0, scalar2=None, op0=ALU.add),
                         reads=[b("i32s")], writes=[b("xqb")])
                for half4 in range(2):
                    fns = [
                        (lambda e, k=k, h4=half4, toc=toc: e.transpose(
                            pb[h4][:, (k % 4) * 128:(k % 4 + 1) * 128],
                            toc[:, k * 128:(k + 1) * 128], ident[:]))
                        for k in range(half4 * 4, half4 * 4 + 4)]
                    P.group("tensor", fns, reads=[b("xqb"), b("ident")],
                            writes=[b(f"pb{half4}")])
                    P.op("scalar",
                         lambda e, h4=half4, oc=oc: e.copy(
                             out=ternT[:].rearrange("p (k m) -> p k m", k=KD)
                             [:, h4 * 4:(h4 + 1) * 4, oc * 128:(oc + 1) * 128],
                             in_=pb[h4][:].rearrange("p (k m) -> p k m", k=4)),
                         reads=[b(f"pb{half4}")], writes=[b("ternT")])

        def x_load(c):
            dst = xb[:].rearrange("p (t d) -> p t d", t=TILES)
            src = dr["x"].rearrange("(c t p) d -> c p t d", c=NCHUNK, t=TILES)[c]
            dma("sync", dst, src, writes=[b("xb")], sem="qx")

        def w1_load(g, slot):
            dst = w1gc[slot][:].rearrange("p (k m) -> p k m", k=KD)
            src = scr["w1s"].rearrange("(k p) m -> p k m", k=KD)[
                :, :, g * 2 * MG:(g + 1) * 2 * MG]
            sm = "w1a" if slot == 0 else "w1b"
            dma("sync", dst, src, reads=[b("w1hD")],
                writes=[b(f"w1gc{slot}")], sem=sm + "h")

        def w2_load(g, slot):
            base = {0: "w2a", 1: "w2b", 2: "w2c", 3: "w2d"}[slot]
            dma("gpsimd", w2gc[slot][:], scr["w2s"][g * 128:(g + 1) * 128, :],
                reads=[b("w2hD")], writes=[b(f"w2gc{slot}")], sem=base + "h")

        x_load(0)

        def yh2(t, k):
            return t[:, k * 2 * CH:k * 2 * CH + CH]

        def yl2(t, k):
            return t[:, k * 2 * CH + CH:(k + 1) * 2 * CH]

        def yhl2(t, k):
            return t[:, k * 2 * CH:(k + 1) * 2 * CH]

        def front_part1(c):
            for tt in range(TILES):
                xt = xb[:, tt * D:(tt + 1) * D]
                P.op("vector", lambda e, xt=xt: e.bn_stats(fbnb[:, 0:6], xt[:, 0:512]),
                     reads=[b("xb")], writes=[b("fbnb")])
                P.op("vector", lambda e, xt=xt: e.bn_stats(fbnb[:, 6:12], xt[:, 512:1024]),
                     reads=[b("xb")], writes=[b("fbnb")])
                P.op("vector", lambda e: e.bn_aggr(fmv[:], fbnb[:]),
                     reads=[b("fbnb")], writes=[b("fmv")])
                P.op("scalar", lambda e: e.activation(out=fs1c[:], in_=fmv[:, 1:2],
                                                      func=AF.Sqrt, scale=1.0 + LN_EPS,
                                                      bias=eps2c[:]),
                     reads=[b("fmv"), b("eps2c")], writes=[b("fs1c")])
                P.op("vector", lambda e, tt=tt: e.reciprocal(
                    out=fr1t[:, tt:tt + 1], in_=fs1c[:]),
                     reads=[b("fs1c")], writes=[b("fr1t")])
                P.op("vector", lambda e, tt=tt: e.tensor_scalar(
                    out=fv1t[:, tt:tt + 1], in0=fmv[:, 0:1],
                    scalar1=fr1t[:, tt:tt + 1], scalar2=-1.0,
                    op0=ALU.mult, op1=ALU.mult),
                     reads=[b("fmv"), b("fr1t")], writes=[b("fv1t")])
                P.op("scalar", lambda e: e.activation(out=fsdc[:], in_=fmv[:, 1:2],
                                                      func=AF.Sqrt, bias=epsc[:]),
                     reads=[b("fmv"), b("epsc")], writes=[b("fsdc")])
                P.op("vector", lambda e, tt=tt: e.reciprocal(
                    out=rct[:, tt:tt + 1], in_=fsdc[:]),
                     reads=[b("fsdc")], writes=[b("rct")])
                P.op("vector", lambda e, tt=tt: e.tensor_scalar(
                    out=nmt[:, tt:tt + 1], in0=fmv[:, 0:1],
                    scalar1=rct[:, tt:tt + 1], scalar2=-1.0,
                    op0=ALU.mult, op1=ALU.mult),
                     reads=[b("fmv"), b("rct")], writes=[b("nmt")])

        def front_part2(c):
            lc = c % CHB
            for tt in range(TILES):
                P.op("scalar", lambda e, tt=tt: e.activation(
                    out=ytmp[:], in_=xb[:, tt * D:(tt + 1) * D], func=AF.Identity,
                    scale=fr1t[:, tt:tt + 1], bias=fv1t[:, tt:tt + 1]),
                     reads=[b("xb"), b("fr1t"), b("fv1t")], writes=[b("ytmp")])
                for h4 in range(2):
                    fns = [(lambda e, k=k, h4=h4: e.transpose(
                        pb[h4][:, (k % 4) * 128:(k % 4 + 1) * 128],
                        ytmp[:, k * 128:(k + 1) * 128], ident[:]))
                        for k in range(h4 * 4, h4 * 4 + 4)]
                    P.group("tensor", fns, reads=[b("ytmp"), b("ident")],
                            writes=[b(f"pb{h4}")])
                    for k in range(h4 * 4, h4 * 4 + 4):
                        P.op("scalar", lambda e, k=k, h4=h4: e.activation(
                            out=tscr[:, (k % 4) * 128:(k % 4 + 1) * 128],
                            in_=pb[h4][:, (k % 4) * 128:(k % 4 + 1) * 128],
                            func=AF.Identity, scale=ln1w[:, k:k + 1],
                            bias=ln1b[:, k:k + 1]),
                             reads=[b(f"pb{h4}"), b("ln1w"), b("ln1b")],
                             writes=[b("tscr")])
                    hdst = y1T2[:].rearrange("p (k t) -> p k t", k=KD)[
                        :, h4 * 4:(h4 + 1) * 4, tt * 128:(tt + 1) * 128]
                    ldst = y1T2[:].rearrange("p (k t) -> p k t", k=KD)[
                        :, h4 * 4:(h4 + 1) * 4, CH + tt * 128:CH + (tt + 1) * 128]
                    tsrc = tscr[:].rearrange("p (k t) -> p k t", k=4)
                    P.op("scalar", lambda e, hdst=hdst, tsrc=tsrc: e.copy(
                        out=hdst, in_=tsrc),
                         reads=[b("tscr")], writes=[b("y1T2")])
                    P.op("vector", lambda e, hdst=hdst, ldst=ldst, tsrc=tsrc:
                         e.tensor_tensor(out=ldst, in0=tsrc, in1=hdst, op=ALU.subtract),
                         reads=[b("tscr"), b("y1T2")], writes=[b("y1T2")])
            for qoff, outsl in ((1024, pb[0][0:NH, 0:CH]), (1040, pb[0][0:NH, CH:2 * CH])):
                fns = []
                for k in range(KD):
                    wh_ap = wqkvh[:, k * 1056 + qoff:k * 1056 + qoff + 16]
                    wl_ap = wqkvl[:, k * 1056 + qoff:k * 1056 + qoff + 16]
                    fns.append(lambda e, o=outsl, w=wh_ap, r=yh2(y1T2, k),
                               st=(k == 0): e.matmul(o, w, r, start=st, stop=False))
                    fns.append(lambda e, o=outsl, w=wh_ap, r=yl2(y1T2, k):
                               e.matmul(o, w, r, start=False, stop=False))
                    fns.append(lambda e, o=outsl, w=wl_ap, r=yh2(y1T2, k),
                               sp=(k == KD - 1): e.matmul(o, w, r, start=False, stop=sp))
                P.group("tensor", fns,
                        reads=[b("wqkvh"), b("wqkvl"), b("y1T2")], writes=[b("pb0")])
            P.op("scalar", lambda e: e.activation(out=qkt[:], in_=pb[0][0:NH, 0:2 * CH],
                                                  func=AF.Tanh),
                 reads=[b("pb0")], writes=[b("x1b")])
            P.op("scalar", lambda e: e.activation(out=cq[:], in_=qkt[:, 0:CH],
                                                  func=AF.Sin, scale=PI / 4,
                                                  bias=halfpi[0:NH, :]),
                 reads=[b("x1b"), b("halfpi")], writes=[b("cq")])
            P.op("scalar", lambda e: e.activation(out=sq[:], in_=qkt[:, 0:CH],
                                                  func=AF.Sin, scale=PI / 4),
                 reads=[b("x1b")], writes=[b("sq")])
            P.op("scalar", lambda e: e.activation(out=coskr[:], in_=qkt[:, CH:2 * CH],
                                                  func=AF.Sin, scale=PI / 4,
                                                  bias=halfpi[0:NH, :]),
                 reads=[b("x1b"), b("halfpi")], writes=[b("coskr")])
            P.op("scalar", lambda e: e.activation(out=sinkr[:], in_=qkt[:, CH:2 * CH],
                                                  func=AF.Sin, scale=PI / 4),
                 reads=[b("x1b")], writes=[b("sinkr")])
            if lc != 0:
                P.op("vector", lambda e: e.tensor_copy(
                    out=carry2[:, 0:1], in_=ckb[:, CH - 1:CH]),
                     reads=[b("ckb")], writes=[b("carry2")])
                P.op("vector", lambda e: e.tensor_copy(
                    out=carry2[:, 1:2], in_=ckb[:, 2 * CH - 1:2 * CH]),
                     reads=[b("ckb")], writes=[b("carry2")])
            for half, src in ((0, "coskr"), (1, "sinkr")):
                srct = {"coskr": coskr, "sinkr": sinkr}[src]
                if lc == 0:
                    P.op("vector", lambda e, h=half, st=srct: e.tensor_tensor_scan(
                        out=ckb[:, h * CH:(h + 1) * CH], data0=onesN[0:NH, :],
                        data1=st[:], initial=0.0, op0=ALU.mult, op1=ALU.add),
                         reads=[b("onesN"), b(src)], writes=[b("ckb")])
                else:
                    P.op("vector", lambda e, h=half, st=srct: e.tensor_tensor_scan(
                        out=scnt[:, h * CH:(h + 1) * CH], data0=onesN[0:NH, :],
                        data1=st[:], initial=0.0, op0=ALU.mult, op1=ALU.add),
                         reads=[b("onesN"), b(src)], writes=[b("x1b")])
                    P.op("vector", lambda e, h=half: e.tensor_scalar(
                        out=ckb[:, h * CH:(h + 1) * CH], in0=scnt[:, h * CH:(h + 1) * CH],
                        scalar1=carry2[:, h:h + 1], scalar2=None, op0=ALU.add),
                         reads=[b("x1b"), b("carry2")], writes=[b("ckb")])
            P.op("vector", lambda e: e.tensor_tensor(out=scnt[:, 0:CH], in0=cq[:],
                                                     in1=ckb[:, 0:CH], op=ALU.mult),
                 reads=[b("cq"), b("ckb")], writes=[b("x1b")])
            P.op("vector", lambda e: e.tensor_tensor(out=scnt[:, CH:2 * CH], in0=sq[:],
                                                     in1=ckb[:, CH:2 * CH], op=ALU.mult),
                 reads=[b("sq"), b("ckb")], writes=[b("x1b")])
            P.op("vector", lambda e: e.scalar_tensor_tensor(
                out=qkt[:, 0:CH], in0=scnt[:, 0:CH], scalar=COS_EPS,
                in1=scnt[:, CH:2 * CH], op0=ALU.add, op1=ALU.add),
                 reads=[b("x1b")], writes=[b("x1b")])
            P.op("vector", lambda e: e.reciprocal(out=qkt[:, CH:2 * CH], in_=qkt[:, 0:CH]),
                 reads=[b("x1b")], writes=[b("x1b")])
            P.op("vector", lambda e: e.tensor_tensor(out=scnt[:, 0:CH], in0=cq[:],
                                                     in1=qkt[:, CH:2 * CH], op=ALU.mult),
                 reads=[b("cq"), b("x1b")], writes=[b("x1b")])
            P.op("vector", lambda e: e.tensor_tensor(out=scnt[:, CH:2 * CH], in0=sq[:],
                                                     in1=qkt[:, CH:2 * CH], op=ALU.mult),
                 reads=[b("sq"), b("x1b")], writes=[b("x1b")])
        def front_part2b(c):
            for bank, off, src, srcbuf in (
                    (0, 0, coskr[:], "coskr"), (0, CH, sinkr[:], "sinkr"),
                    (1, 0, scnt[:, 0:CH], "x1b"), (1, CH, scnt[:, CH:2 * CH], "x1b")):
                P.group("tensor",
                        [lambda e, bank=bank, off=off, src=src: e.matmul(
                            pb[bank][:, off:off + CH], sel[:], src,
                            start=True, stop=True)],
                        reads=[b("sel"), b(srcbuf)], writes=[b(f"pb{bank}")])
            for name, bank, off in [("coskb", 0, 0), ("sinkb", 0, CH),
                                    ("cosqb", 1, 0), ("sinqb", 1, CH)]:
                tgt = {"coskb": coskb, "sinkb": sinkb, "cosqb": cosqb, "sinqb": sinqb}[name]
                P.op("scalar", lambda e, t=tgt, bk=bank, o=off: e.copy(
                    out=t[:], in_=pb[bk][:, o:o + CH]),
                     reads=[b(f"pb{bank}")], writes=[b(name)])
        def middle_vm(c):
            lc = c % CHB
            for fc in range(KD):
                vb = fc % 2
                fns = []
                for k in range(KD):
                    wh_ap = wqkvh[:, k * 1056 + fc * 128:k * 1056 + (fc + 1) * 128]
                    wl_ap = wqkvl[:, k * 1056 + fc * 128:k * 1056 + (fc + 1) * 128]
                    fns.append(lambda e, o=pb[vb][:, 0:2 * CH], w=wh_ap, r=yhl2(y1T2, k),
                               st=(k == 0): e.matmul(o, w, r, start=st, stop=False))
                    fns.append(lambda e, o=pb[vb][:, 0:CH], w=wl_ap, r=yh2(y1T2, k),
                               sp=(k == KD - 1): e.matmul(o, w, r, start=False, stop=sp))
                P.group("tensor", fns,
                        reads=[b("wqkvh"), b("wqkvl"), b("y1T2")],
                        writes=[b(f"pb{vb}")])
                P.op("scalar", lambda e, vb=vb: e.copy(
                    out=t1b[:], in_=pb[vb][:, CH:2 * CH]),
                     reads=[b(f"pb{vb}")], writes=[b("t1b")])
                P.op("vector", lambda e, vb=vb: e.tensor_tensor(
                    out=t2b[:], in0=pb[vb][:, 0:CH], in1=t1b[:], op=ALU.add),
                     reads=[b(f"pb{vb}"), b("t1b")], writes=[b("t2b")])
                if lc == 0:
                    P.op("gpsimd", lambda e: e.tensor_tensor(
                        out=mcos[:], in0=t2b[:], in1=coskb[:], op=ALU.mult),
                         reads=[b("t2b"), b("coskb")], writes=[b("mcos")], sem="gp")
                    P.op("gpsimd", lambda e: e.tensor_tensor(
                        out=msin[:], in0=t2b[:], in1=sinkb[:], op=ALU.mult),
                         reads=[b("t2b"), b("sinkb")], writes=[b("msin")], sem="gp")
                else:
                    P.op("gpsimd", lambda e: e.tensor_tensor(
                        out=mcos[:, 1:CH], in0=t2b[:, 1:CH], in1=coskb[:, 1:CH],
                        op=ALU.mult),
                         reads=[b("t2b"), b("coskb")], writes=[b("mcos")], sem="gp")
                    P.op("vector", lambda e, fc=fc: e.scalar_tensor_tensor(
                        out=mcos[:, 0:1], in0=t2b[:, 0:1], scalar=coskb[:, 0:1],
                        in1=carc[:, fc:fc + 1], op0=ALU.mult, op1=ALU.add),
                         reads=[b("t2b"), b("coskb"), b("carc")], writes=[b("mcos")])
                    P.op("gpsimd", lambda e: e.tensor_tensor(
                        out=msin[:, 1:CH], in0=t2b[:, 1:CH], in1=sinkb[:, 1:CH],
                        op=ALU.mult),
                         reads=[b("t2b"), b("sinkb")], writes=[b("msin")], sem="gp")
                    P.op("vector", lambda e, fc=fc: e.scalar_tensor_tensor(
                        out=msin[:, 0:1], in0=t2b[:, 0:1], scalar=sinkb[:, 0:1],
                        in1=cars[:, fc:fc + 1], op0=ALU.mult, op1=ALU.add),
                         reads=[b("t2b"), b("sinkb"), b("cars")], writes=[b("msin")])
                P.op("vector", lambda e: e.tensor_tensor_scan(
                    out=kvc[:], data0=onesN[:], data1=mcos[:], initial=0.0,
                    op0=ALU.mult, op1=ALU.add),
                     reads=[b("onesN"), b("mcos")], writes=[b("kvc")])
                P.op("vector", lambda e: e.tensor_tensor_scan(
                    out=kvs[:], data0=onesN[:], data1=msin[:], initial=0.0,
                    op0=ALU.mult, op1=ALU.add),
                     reads=[b("onesN"), b("msin")], writes=[b("kvs")])
                if lc != CHB - 1:
                    P.op("vector", lambda e, fc=fc: e.tensor_copy(
                        out=carc[:, fc:fc + 1], in_=kvc[:, CH - 1:CH]),
                         reads=[b("kvc")], writes=[b("carc")])
                    P.op("vector", lambda e, fc=fc: e.tensor_copy(
                        out=cars[:, fc:fc + 1], in_=kvs[:, CH - 1:CH]),
                         reads=[b("kvs")], writes=[b("cars")])
                P.op("gpsimd", lambda e: e.tensor_tensor(out=t1b[:], in0=kvc[:],
                                                         in1=cosqb[:], op=ALU.mult),
                     reads=[b("kvc"), b("cosqb")], writes=[b("t1b")], sem="gp")
                P.op("gpsimd", lambda e: e.tensor_tensor(out=t2b[:], in0=kvs[:],
                                                         in1=sinqb[:], op=ALU.mult),
                     reads=[b("kvs"), b("sinqb")], writes=[b("t2b")], sem="gp")
                P.op("vector", lambda e: e.tensor_tensor(out=hdf[:], in0=t1b[:],
                                                         in1=t2b[:], op=ALU.add),
                     reads=[b("t1b"), b("t2b")], writes=[b("hdf")])
                P.op("scalar", lambda e, fc=fc: e.copy(
                    out=hh2[:, fc * CH:(fc + 1) * CH], in_=hdf[:]),
                     reads=[b("hdf")], writes=[b("hh2")])
                P.op("vector", lambda e, fc=fc: e.tensor_tensor(
                    out=hl2[:, fc * CH:(fc + 1) * CH], in0=hdf[:],
                    in1=hh2[:, fc * CH:(fc + 1) * CH], op=ALU.subtract),
                     reads=[b("hdf"), b("hh2")], writes=[b("hl2")])

        def wo_stage(c):
            fns = []
            for fc in range(KD):
                for tt in range(TILES):
                    sh = hh2[:, fc * CH + tt * 128:fc * CH + (tt + 1) * 128]
                    sl = hl2[:, fc * CH + tt * 128:fc * CH + (tt + 1) * 128]
                    st = (fc == 0)
                    sp = (fc == KD - 1)
                    for dh in range(2):
                        o = pb[2 * tt + dh][:, 0:512]
                        wh_ap = woth[:, fc * D + dh * 512:fc * D + (dh + 1) * 512]
                        wl_ap = wotl[:, fc * D + dh * 512:fc * D + (dh + 1) * 512]
                        fns.append(lambda e, o=o, sh=sh, w=wh_ap, st=st:
                                   e.matmul(o, sh, w, start=st, stop=False))
                        fns.append(lambda e, o=o, sh=sh, w=wl_ap:
                                   e.matmul(o, sh, w, start=False, stop=False))
                    for dh in range(2):
                        o = pb[2 * tt + dh][:, 0:512]
                        wh_ap = woth[:, fc * D + dh * 512:fc * D + (dh + 1) * 512]
                        fns.append(lambda e, o=o, sl=sl, w=wh_ap, sp=sp:
                                   e.matmul(o, sl, w, start=False, stop=sp))
            P.group("tensor", fns, reads=[b("hh2"), b("hl2"), b("woth"), b("wotl")],
                    writes=[b(f"pb{i}") for i in range(4)])

        def a_xa_stage(c):
            for tt in range(TILES):
                P.op("scalar", lambda e, tt=tt: e.activation(
                    out=ytmp[:], in_=xb[:, tt * D:(tt + 1) * D], func=AF.Identity,
                    scale=rct[:, tt:tt + 1], bias=nmt[:, tt:tt + 1]),
                     reads=[b("xb"), b("rct"), b("nmt")], writes=[b("ytmp")])
                for dh in range(2):
                    P.op("vector", lambda e, tt=tt, dh=dh: e.tensor_tensor(
                        out=ab[:, tt * D + dh * 512:tt * D + (dh + 1) * 512],
                        in0=ytmp[:, dh * 512:(dh + 1) * 512],
                        in1=pb[2 * tt + dh][:, 0:512], op=ALU.add),
                         reads=[b("ytmp"), b(f"pb{2 * tt + dh}")], writes=[b("ab")])
            for tt in range(TILES):
                P.op("vector", lambda e, tt=tt: e.tensor_tensor(
                    out=xqb[:, tt * D:(tt + 1) * D], in0=xb[:, tt * D:(tt + 1) * D],
                    in1=ab[:, tt * D:(tt + 1) * D], op=ALU.add),
                     reads=[b("xb"), b("ab")], writes=[b("xqb")])

        def ln2_stage(c):
            for tt in range(TILES):
                at = ab[:, tt * D:(tt + 1) * D]
                P.op("vector", lambda e, at=at: e.bn_stats(bnb[:, 0:6], at[:, 0:512]),
                     reads=[b("ab")], writes=[b("bnb")])
                P.op("vector", lambda e, at=at: e.bn_stats(bnb[:, 6:12], at[:, 512:1024]),
                     reads=[b("ab")], writes=[b("bnb")])
                P.op("vector", lambda e: e.bn_aggr(mv[:], bnb[:]),
                     reads=[b("bnb")], writes=[b("mv")])
                P.op("scalar", lambda e: e.activation(out=sdc[:], in_=mv[:, 1:2],
                                                      func=AF.Sqrt, bias=epsc[:]),
                     reads=[b("mv"), b("epsc")], writes=[b("sdc")])
                P.op("vector", lambda e: e.reciprocal(out=rc[:], in_=sdc[:]),
                     reads=[b("sdc")], writes=[b("rc")])
                P.op("vector", lambda e: e.tensor_scalar(out=nmc[:], in0=mv[:, 0:1],
                                                         scalar1=rc[:], scalar2=-1.0,
                                                         op0=ALU.mult, op1=ALU.mult),
                     reads=[b("mv"), b("rc")], writes=[b("nmc")])
                P.op("scalar", lambda e, at=at: e.activation(
                    out=ytmp[:], in_=at, func=AF.Identity, scale=rc[:], bias=nmc[:]),
                     reads=[b("ab"), b("rc"), b("nmc")], writes=[b("ytmp")])
                for h4 in range(2):
                    fns = [(lambda e, k=k, h4=h4: e.transpose(
                        pb[h4][:, (k % 4) * 128:(k % 4 + 1) * 128],
                        ytmp[:, k * 128:(k + 1) * 128], ident[:]))
                        for k in range(h4 * 4, h4 * 4 + 4)]
                    P.group("tensor", fns, reads=[b("ytmp"), b("ident")],
                            writes=[b(f"pb{h4}")])
                    for k in range(h4 * 4, h4 * 4 + 4):
                        P.op("scalar", lambda e, k=k, h4=h4: e.activation(
                            out=tscr[:, (k % 4) * 128:(k % 4 + 1) * 128],
                            in_=pb[h4][:, (k % 4) * 128:(k % 4 + 1) * 128],
                            func=AF.Identity, scale=ln2w[:, k:k + 1],
                            bias=ln2b[:, k:k + 1]),
                             reads=[b(f"pb{h4}"), b("ln2w"), b("ln2b")],
                             writes=[b("tscr")])
                    hdst = y2T2[:].rearrange("p (k t) -> p k t", k=KD)[
                        :, h4 * 4:(h4 + 1) * 4, tt * 128:(tt + 1) * 128]
                    ldst = y2T2[:].rearrange("p (k t) -> p k t", k=KD)[
                        :, h4 * 4:(h4 + 1) * 4, CH + tt * 128:CH + (tt + 1) * 128]
                    tsrc = tscr[:].rearrange("p (k t) -> p k t", k=4)
                    P.op("scalar", lambda e, hdst=hdst, tsrc=tsrc: e.copy(
                        out=hdst, in_=tsrc),
                         reads=[b("tscr")], writes=[b("y2T2")])
                    P.op("vector", lambda e, hdst=hdst, ldst=ldst, tsrc=tsrc:
                         e.tensor_tensor(out=ldst, in0=tsrc, in1=hdst, op=ALU.subtract),
                         reads=[b("tscr"), b("y2T2")], writes=[b("y2T2")])

        def mlp_stage(c):
            w1_load(0, 0)
            w2_load(0, 0)
            w1_load(1, 1)
            w2_load(1, 1)
            w2_pending = []
            for g in range(NMG):
                gs = g % 2
                gh3 = g % 3
                g4 = g % 4
                w1bank = 2 + gs
                fns = []
                for k in range(KD):
                    wh_ap = w1gc[gs][:, k * 2 * MG:k * 2 * MG + MG]
                    wl_ap = w1gc[gs][:, k * 2 * MG + MG:(k + 1) * 2 * MG]
                    fns.append(lambda e, o=pb[w1bank][:, 0:2 * CH], w=wh_ap,
                               r=yhl2(y2T2, k), st=(k == 0):
                               e.matmul(o, w, r, start=st, stop=False))
                    fns.append(lambda e, o=pb[w1bank][:, 0:CH], w=wl_ap,
                               r=yh2(y2T2, k), sp=(k == KD - 1):
                               e.matmul(o, w, r, start=False, stop=sp))
                P.group("tensor", fns,
                        reads=[b(f"w1gc{gs}"), b("y2T2")],
                        writes=[b(f"pb{w1bank}")])
                P.op("scalar", lambda e, w1bank=w1bank: e.copy(
                    out=mcos[:], in_=pb[w1bank][:, CH:2 * CH]),
                     reads=[b(f"pb{w1bank}")], writes=[b("mcos")])
                P.op("vector", lambda e, w1bank=w1bank: e.tensor_tensor(
                    out=msin[:], in0=pb[w1bank][:, 0:CH], in1=mcos[:], op=ALU.add),
                     reads=[b(f"pb{w1bank}"), b("mcos")], writes=[b("msin")])
                P.op("scalar", lambda e, g=g:
                     e.activation(out=hgf[:], in_=msin[:],
                                  func=AF.Gelu_apprx_tanh, bias=b1c[:, g:g + 1]),
                     reads=[b("msin"), b("b1c")], writes=[b("hgf")])
                P.op("scalar", lambda e, gh3=gh3: e.copy(out=hgh[gh3][:], in_=hgf[:]),
                     reads=[b("hgf")], writes=[b(f"hgh{gh3}")])
                P.op("vector", lambda e, gh3=gh3: e.tensor_tensor(
                    out=hgl[gh3][:], in0=hgf[:], in1=hgh[gh3][:], op=ALU.subtract),
                     reads=[b("hgf"), b(f"hgh{gh3}")], writes=[b(f"hgl{gh3}")])
                fns = []
                for tt in range(TILES):
                    sh = hgh[gh3][:, tt * 128:(tt + 1) * 128]
                    sl = hgl[gh3][:, tt * 128:(tt + 1) * 128]
                    st = (g == 0)
                    for dh in range(2):
                        o = pb[4 + 2 * tt + dh][:, 0:512]
                        wh_ap = w2gc[g4][:, dh * 512:(dh + 1) * 512]
                        wl_ap = w2gc[g4][:, D + dh * 512:D + (dh + 1) * 512]
                        fns.append(lambda e, o=o, sh=sh, w=wh_ap, st=st:
                                   e.matmul(o, sh, w, start=st, stop=False))
                        fns.append(lambda e, o=o, sh=sh, w=wl_ap:
                                   e.matmul(o, sh, w, start=False, stop=False))
                    for dh in range(2):
                        o = pb[4 + 2 * tt + dh][:, 0:512]
                        wh_ap = w2gc[g4][:, dh * 512:(dh + 1) * 512]
                        fns.append(lambda e, o=o, sl=sl, w=wh_ap:
                                   e.matmul(o, sl, w, start=False, stop=False))
                wrt = [b(f"pb{4 + i}") for i in range(4)] if g == 0 else []
                w2_pending.append((fns,
                                   [b(f"hgh{gh3}"), b(f"hgl{gh3}"),
                                    b(f"w2gc{g4}")], wrt))
                if g >= 2:
                    fns2, rds2, wrt2 = w2_pending.pop(0)
                    P.group("tensor", fns2, reads=rds2, writes=wrt2)
                if g + 2 < NMG:
                    w1_load(g + 2, gs)
                    w2_load(g + 2, (g + 2) % 4)
                if g == 1 and c + 1 < NCHUNK:
                    front_part1(c + 1)
                if g == 6 and c + 1 < NCHUNK:
                    front_part2(c + 1)
                if g == 10 and c + 1 < NCHUNK:
                    front_part2b(c + 1)
                if g == 22 and c + 1 < NCHUNK:
                    middle_vm(c + 1)
            while w2_pending:
                fns2, rds2, wrt2 = w2_pending.pop(0)
                P.group("tensor", fns2, reads=rds2, writes=wrt2)
            fns = []
            for tt in range(TILES):
                for dh in range(2):
                    fns.append(lambda e, tt=tt, dh=dh: e.matmul(
                        pb[4 + 2 * tt + dh][:, 0:512], onesrow[:, 0:128],
                        b2row[:, dh * 512:(dh + 1) * 512], start=False, stop=True))
            P.group("tensor", fns, reads=[b("onesrow"), b("b2row")],
                    writes=[b(f"pb{4 + i}") for i in range(4)])

        def x1_stage(c):
            for tt in range(TILES):
                for dh in range(2):
                    P.op("vector", lambda e, tt=tt, dh=dh: e.tensor_tensor(
                        out=ab[:, tt * D + dh * 512:tt * D + (dh + 1) * 512],
                        in0=xqb[:, tt * D + dh * 512:tt * D + (dh + 1) * 512],
                        in1=pb[4 + 2 * tt + dh][:, 0:512], op=ALU.add),
                         reads=[b("xqb"), b(f"pb{4 + 2 * tt + dh}")], writes=[b("ab")])

        def ln3_quant_tile(c, tt):
            if True:
                x1t = ab[:, tt * D:(tt + 1) * D]
                P.op("vector", lambda e, x1t=x1t: e.bn_stats(bnb[:, 0:6], x1t[:, 0:512]),
                     reads=[b("ab")], writes=[b("bnb")])
                P.op("vector", lambda e, x1t=x1t: e.bn_stats(bnb[:, 6:12], x1t[:, 512:1024]),
                     reads=[b("ab")], writes=[b("bnb")])
                P.op("vector", lambda e: e.bn_aggr(mv[:], bnb[:]),
                     reads=[b("bnb")], writes=[b("mv")])
                P.op("scalar", lambda e: e.activation(out=sdc[:], in_=mv[:, 1:2],
                                                      func=AF.Sqrt, bias=epsc[:]),
                     reads=[b("mv"), b("epsc")], writes=[b("sdc")])
                P.op("vector", lambda e: e.reciprocal(out=rc[:], in_=sdc[:]),
                     reads=[b("sdc")], writes=[b("rc")])
                P.op("vector", lambda e: e.tensor_scalar(out=nmc[:], in0=mv[:, 0:1],
                                                         scalar1=rc[:], scalar2=-1.0,
                                                         op0=ALU.mult, op1=ALU.mult),
                     reads=[b("mv"), b("rc")], writes=[b("nmc")])
                P.op("scalar", lambda e, x1t=x1t: e.activation(
                    out=ytmp[:], in_=x1t, func=AF.Identity, scale=rc[:], bias=nmc[:]),
                     reads=[b("ab"), b("rc"), b("nmc")], writes=[b("ytmp")])
                P.op("vector", lambda e: e.tensor_reduce(
                    out=amaxc[:], in_=ytmp[:], axis=AX.X, op=ALU.max,
                    apply_absolute_value=True),
                     reads=[b("ytmp")], writes=[b("amaxc")])
                P.op("vector", lambda e: e.tensor_scalar(
                    out=mxc[:], in0=amaxc[:], scalar1=1e-5, scalar2=None, op0=ALU.max),
                     reads=[b("amaxc")], writes=[b("mxc")])
                P.op("vector", lambda e: e.reciprocal(out=rcpmc[:], in_=mxc[:]),
                     reads=[b("mxc")], writes=[b("rcpmc")])
                P.op("vector", lambda e, tt=tt: e.tensor_scalar(
                    out=rsc[tt][:], in0=mxc[:], scalar1=fscale[:], scalar2=1.0 / 127.0,
                    op0=ALU.mult, op1=ALU.mult),
                     reads=[b("mxc"), b("fscale")], writes=[b(f"rsc{tt}")])
                P.op("vector", lambda e: e.tensor_scalar(
                    out=tmpA[:], in0=ytmp[:], scalar1=rcpmc[:], scalar2=127.0,
                    op0=ALU.mult, op1=ALU.mult),
                     reads=[b("ytmp"), b("rcpmc")], writes=[b("tmpA")])
                P.op("vector", lambda e: e.tensor_copy(out=p2h[:], in_=tmpA[:]),
                     reads=[b("tmpA")], writes=[b("p2h")])
                P.op("vector", lambda e: e.tensor_tensor(
                    out=ytmp[:], in0=tmpA[:], in1=p2h[:], op=ALU.subtract),
                     reads=[b("tmpA"), b("p2h")], writes=[b("ytmp")])
                P.op("vector", lambda e: e.scalar_tensor_tensor(
                    out=tmpA[:], in0=ytmp[:], scalar=0.5, in1=p2h[:],
                    op0=ALU.is_gt, op1=ALU.add),
                     reads=[b("ytmp"), b("p2h")], writes=[b("tmpA")])
                P.op("vector", lambda e: e.tensor_scalar(
                    out=p2h[:], in0=ytmp[:], scalar1=-0.5,
                    scalar2=None, op0=ALU.is_lt),
                     reads=[b("ytmp")], writes=[b("p2h")])
                P.op("vector", lambda e, tt=tt: e.tensor_tensor(
                    out=xqb[:, tt * D:(tt + 1) * D], in0=tmpA[:], in1=p2h[:],
                    op=ALU.subtract),
                     reads=[b("tmpA"), b("p2h")], writes=[b("xqb")])

        def xq_trans_stage(c):
            for tt in range(TILES):
                for h4 in range(2):
                    tb = 4 + h4
                    fns = [(lambda e, k=k, tb=tb, tt=tt: e.transpose(
                        pb[tb][:, (k % 4) * 128:(k % 4 + 1) * 128],
                        xqb[:, tt * D + k * 128:tt * D + (k + 1) * 128], ident[:]))
                        for k in range(h4 * 4, h4 * 4 + 4)]
                    P.group("tensor", fns, reads=[b("xqb"), b("ident")],
                            writes=[b(f"pb{tb}")])
                    for k in range(h4 * 4, h4 * 4 + 4):
                        P.op("scalar", lambda e, k=k, tb=tb, tt=tt: e.copy(
                            out=xqT[:, k * CH + tt * 128:k * CH + (tt + 1) * 128],
                            in_=pb[tb][:, (k % 4) * 128:(k % 4 + 1) * 128]),
                             reads=[b(f"pb{tb}")], writes=[b("xqT")])

        def x1f_stage(c):
            for tt in range(TILES):
                P.op("vector", lambda e, tt=tt: e.tensor_tensor(
                    out=x1b[:, tt * D:(tt + 1) * D],
                    in0=ab[:, tt * D:(tt + 1) * D], in1=fbias[:], op=ALU.add),
                     reads=[b("ab"), b("fbias")], writes=[b("x1b")])

        def ffn_out_stage(c):
            ffnbank = {0: (6, 7), 1: (4, 5)}
            stagebuf = {0: (tmpA, "tmpA"), 1: (ytmp, "ytmp")}
            for tt in range(TILES):
                fns = []
                for dh in range(2):
                    for k in range(KD):
                        fns.append(lambda e, o=pb[ffnbank[tt][dh]][:, 0:512],
                                   tt=tt, dh=dh, k=k: e.matmul(
                            o,
                            xqT[:, k * CH + tt * 128:k * CH + (tt + 1) * 128],
                            ternT[:, k * D + dh * 512:k * D + (dh + 1) * 512],
                            start=(k == 0), stop=(k == KD - 1)))
                P.group("tensor", fns, reads=[b("xqT"), b("ternT")],
                        writes=[b(f"pb{ffnbank[tt][0]}"), b(f"pb{ffnbank[tt][1]}")])
                sbuf_t, sname = stagebuf[tt]
                for dh in range(2):
                    P.op("vector", lambda e, tt=tt, dh=dh, sbuf_t=sbuf_t:
                         e.scalar_tensor_tensor(
                        out=sbuf_t[:, dh * 512:(dh + 1) * 512],
                        in0=pb[ffnbank[tt][dh]][:, 0:512], scalar=rsc[tt][:],
                        in1=x1b[:, tt * D + dh * 512:tt * D + (dh + 1) * 512],
                        op0=ALU.mult, op1=ALU.add),
                         reads=[b(f"pb{ffnbank[tt][dh]}"), b(f"rsc{tt}"), b("x1b")],
                         writes=[b(sname)])
                dst = out_d.rearrange("(c t p) d -> c p t d",
                                      c=NCHUNK, t=TILES)[c, :, tt, :]
                dma("gpsimd", dst, sbuf_t[:], reads=[b(sname)],
                    sem="qo" if tt == 0 else "qo2")

        front_part1(0)
        front_part2(0)
        front_part2b(0)
        middle_vm(0)
        wo_stage(0)
        a_xa_stage(0)
        x_load(1)
        for c in range(NCHUNK):
            ln2_stage(c)
            mlp_stage(c)
            x1_stage(c)
            if c + 1 < NCHUNK:
                wo_stage(c + 1)
            ln3_quant_tile(c, 0)
            ln3_quant_tile(c, 1)
            x1f_stage(c)
            xq_trans_stage(c)
            if c + 1 < NCHUNK:
                a_xa_stage(c + 1)
                if c + 2 < NCHUNK:
                    x_load(c + 2)
            ffn_out_stage(c)

        P.items["gpsimd"].append(("w", "qo", P.cnt["qo"]))
        P.items["gpsimd"].append(("w", "qo2", P.cnt["qo2"]))

        with nc.Block() as block:
            def replay(eng_obj, name):
                for it in P.items[name]:
                    if it[0] == "w":
                        eng_obj.wait_ge(sems[it[1]], it[2])
                    else:
                        instr = it[1](eng_obj)
                        if it[2] is not None:
                            instr.then_inc(sems[it[2]], it[3])

            @block.sync
            def _(sync):
                replay(sync, "sync")

            @block.tensor
            def _(tensor):
                replay(tensor, "tensor")

            @block.vector
            def _(vector):
                replay(vector, "vector")

            @block.scalar
            def _(scalar):
                replay(scalar, "scalar")

            @block.gpsimd
            def _(gpsimd):
                replay(gpsimd, "gpsimd")

    return nc


def kernel(**inputs):
    in_maps = _prep_inputs(inputs)
    nc = build_program()
    res = run_bass_kernel_spmd(nc, in_maps, core_ids=list(range(NCORES)))
    out = np.concatenate(
        [np.asarray(r["out"]).reshape(B_PER_CORE, T, D) for r in res.results], axis=0)
    return out
